# version 08 (final)
# baseline (speedup 1.0000x reference)
import math
import contextlib
import numpy as np
import ml_dtypes

import concourse.bass as bass
import concourse.mybir as mybir
from concourse.bass_utils import run_bass_kernel_spmd

F32 = mybir.dt.float32
BF16 = mybir.dt.bfloat16
AF = mybir.ActivationFunctionType
ALU = mybir.AluOpType
AX = mybir.AxisListType

N_CORES = 8
NB = 4
T = 2048
D = 1024
DC = D // 128
TC = T // 128
E = 8
LN_EPS = 1e-5
ATT_SCALE = 1.0 / math.sqrt(T)
G_CENTER = 1.0 / math.e
G_LOWER = 0.5 - math.log(4.0 / 3.0)
G_UPPER = 0.5

DEBUG_TAPS = False

ENGINES = ("sync", "gpsimd", "tensor", "vector", "scalar")


class Plan:

    def __init__(self):
        self.stream = {e: [] for e in ENGINES}
        self.cnt = {}

    def val(self, sem):
        return self.cnt.get(id(sem), 0)

    def wait(self, eng, sem, v=None):
        if v is None:
            v = self.val(sem)
        if v <= 0:
            return
        self.stream[eng].append(("w", sem, v))

    def drain(self, eng):
        self.stream[eng].append(("o", lambda e: e.drain(), None, 1))

    def op(self, eng, fn, sem=None, n=1):
        v = None
        if sem is not None:
            v = self.val(sem) + n
            self.cnt[id(sem)] = v
        self.stream[eng].append(("o", fn, sem, n))
        return v

    def replay(self, eng_name, h):
        for it in self.stream[eng_name]:
            if it[0] == "w":
                h.wait_ge(it[1], it[2])
            else:
                ins = it[1](h)
                if it[2] is not None:
                    ins.then_inc(it[2], it[3])


def build_nc(nb=NB, debug_taps=False, skip_ffn=False):
    nc = bass.Bass("TRN2", target_bir_lowering=False)

    dx = nc.dram_tensor("x", [nb, T, D], F32, kind="ExternalInput").ap()
    dqkvT = nc.dram_tensor("qkvT", [D, 3 * D], F32, kind="ExternalInput").ap()
    dprojT = nc.dram_tensor("projT", [D, D], F32, kind="ExternalInput").ap()
    dwg = nc.dram_tensor("wg", [E, D, D], F32, kind="ExternalInput").ap()
    dwu = nc.dram_tensor("wu", [E, D, D], F32, kind="ExternalInput").ap()
    dwd = nc.dram_tensor("wd", [E, D, D], F32, kind="ExternalInput").ap()
    dwrT = nc.dram_tensor("wrT", [E, D], F32, kind="ExternalInput").ap()
    dnormw = nc.dram_tensor("normw", [1, D], F32, kind="ExternalInput").ap()
    dnormb = nc.dram_tensor("normb", [1, D], F32, kind="ExternalInput").ap()
    dqkvb = nc.dram_tensor("qkvb", [1, 3 * D], F32, kind="ExternalInput").ap()
    dprojb = nc.dram_tensor("projb", [1, D], F32, kind="ExternalInput").ap()
    dtemp = nc.dram_tensor("temp", [1, 1], F32, kind="ExternalInput").ap()
    didenb = nc.dram_tensor("idenb", [128, 128], BF16, kind="ExternalInput").ap()
    didenf = nc.dram_tensor("idenf", [128, 128], F32, kind="ExternalInput").ap()
    donesb = nc.dram_tensor("onesb", [1, 512], BF16, kind="ExternalInput").ap()
    donesf = nc.dram_tensor("onesf", [1, 128], F32, kind="ExternalInput").ap()
    dout = nc.dram_tensor("out", [nb, T, D], F32, kind="ExternalOutput").ap()

    taps = {}
    if debug_taps:
        taps["xnT"] = nc.dram_tensor("t_xnT", [128, 8192], F32, kind="ExternalOutput").ap()
        taps["ln2T"] = nc.dram_tensor("t_ln2T", [128, 8192], F32, kind="ExternalOutput").ap()
        taps["rout"] = nc.dram_tensor("t_rout", [128, 128], F32, kind="ExternalOutput").ap()
        taps["stats"] = nc.dram_tensor("t_stats", [128, 16], F32, kind="ExternalOutput").ap()
        taps["sumd"] = nc.dram_tensor("t_sumd", [128, 16], F32, kind="ExternalOutput").ap()
        taps["sqd"] = nc.dram_tensor("t_sqd", [128, 16], F32, kind="ExternalOutput").ap()
        taps["inh"] = nc.dram_tensor("t_inh", [128, 128], F32, kind="ExternalOutput").ap()
        taps["ps6"] = nc.dram_tensor("t_ps6", [128, 128], F32, kind="ExternalOutput").ap()
        taps["rt"] = nc.dram_tensor("t_rt", [128, 1], F32, kind="ExternalOutput").ap()
        taps["lot"] = nc.dram_tensor("t_lot", [128, 512], F32, kind="ExternalOutput").ap()
        taps["wrf"] = nc.dram_tensor("t_wrf", [128, 64], F32, kind="ExternalOutput").ap()
        taps["wrhl"] = nc.dram_tensor("t_wrhl", [128, 32], F32, kind="ExternalOutput").ap()
        taps["nwt"] = nc.dram_tensor("t_nwt", [128, 8], F32, kind="ExternalOutput").ap()
        taps["z0"] = nc.dram_tensor("t_z0", [128, 8], F32, kind="ExternalOutput").ap()

    with contextlib.ExitStack() as ctx:
        sb = lambda name, shape, dt: ctx.enter_context(nc.sbuf_tensor(name, shape, dt))
        ps = lambda name: ctx.enter_context(nc.psum_tensor(name, [128, 512], F32))
        sem = lambda name: ctx.enter_context(nc.semaphore(name))

        AR1 = sb("ar1", [128, 16384], BF16)
        AR2 = sb("ar2", [128, 16384], BF16)
        AR3 = sb("ar3", [128, 16384], BF16)
        AR4 = sb("ar4", [128, 16384], BF16)
        AR5 = sb("ar5", [128, 16384], BF16)
        AR6 = sb("ar6", [128, 16384], BF16)

        AR4f = AR4[:].bitcast(F32)
        AR5f = AR5[:].bitcast(F32)
        AR6f = AR6[:].bitcast(F32)

        IDB = sb("idb", [128, 128], BF16)
        IDF = sb("idf", [128, 128], F32)
        ROWS = sb("rows", [128, 4608], BF16)
        NWT = sb("nwt", [128, 8], F32)
        NWTQ = sb("nwtq", [128, 8], F32)
        NBTB = sb("nbtb", [128, 8], BF16)
        WRH = sb("wrh", [128, 64], BF16)
        WRL = sb("wrl", [128, 64], BF16)
        WRF = sb("wrf", [128, 64], F32)
        RT = sb("rt", [128, 1], F32)
        ST_r = sb("st_r", [128, 16], F32)
        ST_b1 = sb("st_b1", [128, 16], F32)
        ST_us = sb("st_us", [128, 16], F32)
        ST_r2 = sb("st_r2", [128, 16], F32)
        ST_b2 = sb("st_b2", [128, 16], F32)
        ST_t = sb("st_t", [128, 16], F32)
        SMX = sb("smx", [128, 16], F32)
        SSUM = sb("ssum", [128, 16], F32)
        SNMX = sb("snmx", [128, 8], F32)
        SRCP = sb("srcp", [128, 8], F32)
        SUMD = sb("sumd", [128, 16], F32)
        SQD = sb("sqd", [128, 16], F32)
        RW = sb("rw", [128, 128], F32)
        RZ = sb("rz", [128, 128], F32)
        RS = sb("rs", [128, 768], F32)

        PS = [ps(f"ps{i}") for i in range(8)]

        s_x = sem("s_x")
        s_xc = [sem(f"s_xc{i}") for i in range(4)]
        s_wa = sem("s_wa")
        s_wb = sem("s_wb")
        s_wg = sem("s_wg")
        s_wd = sem("s_wd")
        s_wp = sem("s_wp")
        s_xr = [sem("s_xr0"), sem("s_xr1")]
        s_t = sem("s_t")
        s_v = sem("s_v")
        s_s = sem("s_s")
        s_o = sem("s_o")

        p = Plan()

        def col(tile, c):
            return tile[:, c:c + 1]

        x_r = dx.rearrange("b (n p) m -> b p n m", p=128)
        out_r = dout.rearrange("b (n p) m -> b p n m", p=128)
        qkvT_r = dqkvT.rearrange("(dc p) f -> p dc f", p=128)
        projT_r = dprojT.rearrange("(dc p) f -> p dc f", p=128)
        wg_r = dwg.rearrange("e (dc p) i -> e p dc i", p=128)
        wu_r = dwu.rearrange("e (dc p) i -> e p dc i", p=128)
        wd_r = dwd.rearrange("e (ic p) d -> e p ic d", p=128)

        def ar3v(c0, c1):
            return AR3[:, c0:c1]

        XCH = [AR4[:, 8192 + i * 2048: 8192 + (i + 1) * 2048].bitcast(F32) for i in range(4)]
        CHXF = [AR4[:, i * 2048: (i + 1) * 2048].bitcast(F32) for i in range(2)]
        CHLN = [AR4[:, 4096 + i * 1024: 4096 + (i + 1) * 1024] for i in range(2)]
        LOT = [AR4[:, 6144 + i * 1024: 6144 + (i + 1) * 1024] for i in range(2)]

        def swT(ki, qi):
            return AR2[:, ki * 1024 + qi * 128: ki * 1024 + (qi + 1) * 128]

        ATB = [AR2[:, 8192 + i * 4096: 8192 + (i + 1) * 4096] for i in range(2)]

        GS = AR3[:, 0:8192]
        US = AR3[:, 8192:16384]
        DS = AR2[:, 0:8192]
        HB = AR2[:, 8192:12288]
        SILU = [AR2[:, 12288 + i * 512: 12288 + (i + 1) * 512] for i in range(2)]

        WPJ = AR4[:, 0:8192]
        XRS = [AR4[:, 8192 + i * 2048: 8192 + (i + 1) * 2048].bitcast(F32) for i in range(2)]
        URS = [AR4[:, 12288 + i * 2048: 12288 + (i + 1) * 2048].bitcast(F32) for i in range(2)]

        WB = [AR6[:, 0:8192], AR6[:, 8192:16384]]

        def accv(cc):
            a = AR5f if cc < 8 else AR6f
            return a[:, (cc % 8) * 1024: (cc % 8 + 1) * 1024]

        ROW_BIAS = ROWS[0:1, 0:3072]
        ROW_ONES = ROWS[0:1, 3072:3584]
        ROW_PROJB = ROWS[0:1, 3584:4608]
        ROW_ONE64 = ROWS[64:65, 0:1]
        ROW_QKVB = ROWS[64:65, 1:3073]

        stage = AR4f
        st_wr = stage[0:8, 0:1024]
        st_nw = stage[0:1, 1024:2048]
        st_nb = stage[0:1, 2048:3072]
        st_pb = stage[0:1, 3072:4096]
        st_qb = stage[0:1, 4096:7168]
        st_tp = stage[0:1, 7168:7169]
        st_rc = stage[0:1, 7169:7170]
        st_of = stage[0:1, 7296:7424]
        st_wh = stage[1:2, 0:64]
        st_whf = stage[0:1, 7424:7488]

        p.op("sync", lambda e: e.dma_start(out=IDB[:], in_=didenb[:]), s_x, 16)
        p.op("sync", lambda e: e.dma_start(out=IDF[:], in_=didenf[:]), s_x, 16)
        p.op("sync", lambda e: e.dma_start(out=st_wr, in_=dwrT[:]), s_x, 16)
        p.op("sync", lambda e: e.dma_start(out=st_nw, in_=dnormw[:]), s_x, 16)
        p.op("sync", lambda e: e.dma_start(out=st_nb, in_=dnormb[:]), s_x, 16)
        p.op("sync", lambda e: e.dma_start(out=st_pb, in_=dprojb[:]), s_x, 16)
        p.op("sync", lambda e: e.dma_start(out=st_qb, in_=dqkvb[:]), s_x, 16)
        p.op("sync", lambda e: e.dma_start(out=st_tp, in_=dtemp[:]), s_x, 16)
        p.op("sync", lambda e: e.dma_start(out=ROW_ONES, in_=donesb[:]), s_x, 16)
        p.op("sync", lambda e: e.dma_start(out=ROW_ONE64, in_=donesb[0:1, 0:1]), s_x, 16)
        p.op("sync", lambda e: e.dma_start(out=st_of, in_=donesf[:]), s_x, 16)
        v_pro_loads = p.val(s_x)

        p.wait("tensor", s_x, v_pro_loads)
        for dc in range(8):
            src = st_nw[:, dc * 128:(dc + 1) * 128]
            p.op("tensor", lambda e, s=src, d=PS[0][:, dc:dc + 1]: e.transpose(d, s, IDF[0:1, 0:1]))
        for dc in range(8):
            src = st_nb[:, dc * 128:(dc + 1) * 128]
            p.op("tensor", lambda e, s=src, d=PS[0][:, 8 + dc:9 + dc]: e.transpose(d, s, IDF[0:1, 0:1]))
        for dc in range(8):
            src = st_wr[:, dc * 128:(dc + 1) * 128]
            p.op("tensor", lambda e, s=src, d=PS[1][:, dc * 8:(dc + 1) * 8]: e.transpose(d, s, IDF[0:8, 0:8]))
        p.wait("vector", s_x, v_pro_loads)
        p.op("vector", lambda e: e.reciprocal(st_rc, st_tp), s_v, 1)
        v_rc = p.val(s_v)
        p.wait("tensor", s_v, v_rc)
        vt_pro = p.op("tensor", lambda e: e.matmul(PS[2][:, 0:1], st_of, st_rc, start=True, stop=True), s_t, 1)

        p.wait("vector", s_t, vt_pro)
        p.op("vector", lambda e: e.tensor_copy(NWT[:], PS[0][:, 0:8]))
        p.op("vector", lambda e: e.tensor_scalar(NWTQ[:], PS[0][:, 0:8], ATT_SCALE, None, op0=ALU.mult))
        p.op("vector", lambda e: e.tensor_copy(NBTB[:], PS[0][:, 8:16]))
        p.op("vector", lambda e: e.tensor_copy(WRF[:], PS[1][:, 0:64]))
        p.op("vector", lambda e: e.tensor_copy(RT[:], PS[2][:, 0:1]))
        p.drain("vector")
        p.op("vector", lambda e: e.tensor_copy(WRH[:], WRF[:]))
        p.drain("vector")
        p.op("vector", lambda e: e.tensor_copy(RS[:, 0:64], WRH[:]))
        p.drain("vector")
        p.op("vector", lambda e: e.tensor_tensor(WRL[:], WRF[:], RS[:, 0:64], op=ALU.subtract))
        p.op("vector", lambda e: e.tensor_copy(ROW_PROJB, st_pb))
        p.op("vector", lambda e: e.tensor_copy(ROW_QKVB, st_qb), s_v, 1)
        v_pro_done = p.val(s_v)
        if debug_taps:
            p.wait("sync", s_v, v_pro_done)
            p.op("sync", lambda e: e.dma_start(out=taps["wrf"], in_=WRF[:]), s_o, 16)
            p.op("sync", lambda e: e.dma_start(out=taps["wrhl"], in_=WRH[:].bitcast(F32)), s_o, 16)
            p.op("sync", lambda e: e.dma_start(out=taps["nwt"], in_=NWT[:]), s_o, 16)

        tags = {}

        for b in range(nb):
            xslot_tag = [None] * 4
            lot_tag = [None, None]
            xdma_v = [0] * TC

            def emit_xdma(cc):
                sl = cc % 4
                if xslot_tag[sl] is not None:
                    p.wait("sync", *xslot_tag[sl])
                if cc == 0 and b == 0:
                    p.wait("sync", s_v, v_pro_done)
                if cc == 0 and b > 0 and tags.get("ar4_free") is not None:
                    p.wait("sync", *tags["ar4_free"])
                xdma_v[cc] = p.op(
                    "sync",
                    lambda e, d=XCH[sl], s=x_r[b, :, cc, :]: e.dma_start(out=d, in_=s),
                    s_xc[sl], 16)

            pre_x = tags.pop("pre_x", None)
            if pre_x is not None:
                for i4 in range(4):
                    xdma_v[i4] = pre_x[i4]
            else:
                emit_xdma(0)
                emit_xdma(1)
                emit_xdma(2)
                emit_xdma(3)
            for pp in range(TC // 2):
                ca, cbk = 2 * pp, 2 * pp + 1
                if cbk + 2 < TC and pp >= 1:
                    emit_xdma(cbk + 1)
                    emit_xdma(cbk + 2)
                two = ((ca, 0, XCH[ca % 4], CHXF[0], CHLN[0], LOT[0]),
                       (cbk, 8, XCH[cbk % 4], CHXF[1], CHLN[1], LOT[1]))

                def c6(base, j):
                    return ST_t[:, base + j: base + j + 1]

                for (cc, sb_, xch, xf, lnb, lo) in two:
                    p.wait("vector", s_xc[cc % 4], xdma_v[cc])
                    p.op("vector", lambda e, x_=xch, o=c6(sb_, 0): e.reduce_sum(o, x_, axis=AX.X))
                for (cc, sb_, xch, xf, lnb, lo) in two:
                    p.wait("scalar", s_xc[cc % 4], xdma_v[cc])
                    if tags.get(f"chxf{cc % 2}") is not None:
                        p.wait("scalar", *tags[f"chxf{cc % 2}"])
                    p.op("scalar", lambda e, x_=xch, o=xf, a=c6(sb_, 1): e.activation(
                        o, x_, AF.Square, accum_out=a), s_s, 1)
                v_sq = p.val(s_s)
                p.wait("vector", s_s, v_sq)
                p.drain("vector")
                for (cc, sb_, xch, xf, lnb, lo) in two:
                    p.op("vector", lambda e, s=c6(sb_, 0), o=c6(sb_, 2): e.tensor_scalar(
                        o, s, 1.0 / D, None, op0=ALU.mult))
                for (cc, sb_, xch, xf, lnb, lo) in two:
                    p.op("vector", lambda e, s=c6(sb_, 1), o=c6(sb_, 3): e.tensor_scalar(
                        o, s, 1.0 / D, None, op0=ALU.mult))
                for (cc, sb_, xch, xf, lnb, lo) in two:
                    p.op("vector", lambda e, m=c6(sb_, 2), q=c6(sb_, 3), o=c6(sb_, 4):
                         e.scalar_tensor_tensor(o, m, m, q, op0=ALU.mult, op1=ALU.subtract))
                for (cc, sb_, xch, xf, lnb, lo) in two:
                    p.op("vector", lambda e, v_=c6(sb_, 4): e.tensor_scalar(
                        v_, v_, -1.0, LN_EPS, op0=ALU.mult, op1=ALU.add),
                         s_v if cc == cbk else None, 1)
                v_var = p.val(s_v)
                p.wait("scalar", s_v, v_var)
                for (cc, sb_, xch, xf, lnb, lo) in two:
                    p.op("scalar", lambda e, v_=c6(sb_, 4), o=c6(sb_, 5): e.activation(
                        o, v_, AF.Sqrt), s_s if cc == cbk else None, 1)
                v_sd = p.val(s_s)
                p.wait("vector", s_s, v_sd)
                for (cc, sb_, xch, xf, lnb, lo) in two:
                    p.op("vector", lambda e, s=c6(sb_, 5), o=col(ST_r, cc): e.reciprocal(o, s))
                for (cc, sb_, xch, xf, lnb, lo) in two:
                    p.op("vector", lambda e, m=c6(sb_, 2), r=col(ST_r, cc), o=col(ST_b1, cc):
                         e.scalar_tensor_tensor(o, m, -1.0, r, op0=ALU.mult, op1=ALU.mult))
                for (cc, sb_, xch, xf, lnb, lo) in two:
                    p.op("vector", lambda e, r=col(ST_r, cc), o=col(ST_us, cc): e.tensor_scalar(
                        o, r, 1.0, None, op0=ALU.add), s_v if cc == cbk else None, 1)
                v_st1 = p.val(s_v)
                p.wait("scalar", s_v, v_st1)
                for (cc, sb_, xch, xf, lnb, lo) in two:
                    p.op("scalar", lambda e, x_=xch, o=xf, c=cc: e.activation(
                        o, x_, AF.Identity, bias=col(ST_b1, c), scale=col(ST_r, c)),
                         s_s, 1)
                    xslot_tag[cc % 4] = (s_s, p.val(s_s))
                v_xn = p.val(s_s)

                p.wait("vector", s_s, v_xn)
                for (cc, sb_, xch, xf, lnb, lo) in two:
                    p.op("vector", lambda e, x_=xf, o=c6(sb_, 0): e.reduce_sum(o, x_, axis=AX.X))
                for (cc, sb_, xch, xf, lnb, lo) in two:
                    if tags.get(f"chln{cc % 2}") is not None:
                        p.wait("scalar", *tags[f"chln{cc % 2}"])
                    p.op("scalar", lambda e, x_=xf, o=lnb, a=c6(sb_, 1): e.activation(
                        o, x_, AF.Square, accum_out=a), s_s, 1)
                v_sq2 = p.val(s_s)
                p.wait("vector", s_s, v_sq2)
                p.drain("vector")
                for (cc, sb_, xch, xf, lnb, lo) in two:
                    p.op("vector", lambda e, s=c6(sb_, 0), o=c6(sb_, 2): e.tensor_scalar(
                        o, s, 1.0 / D, None, op0=ALU.mult))
                for (cc, sb_, xch, xf, lnb, lo) in two:
                    p.op("vector", lambda e, s=c6(sb_, 1), o=c6(sb_, 3): e.tensor_scalar(
                        o, s, 1.0 / D, None, op0=ALU.mult))
                for (cc, sb_, xch, xf, lnb, lo) in two:
                    p.op("vector", lambda e, m=c6(sb_, 2), q=c6(sb_, 3), o=c6(sb_, 4):
                         e.scalar_tensor_tensor(o, m, m, q, op0=ALU.mult, op1=ALU.subtract))
                for (cc, sb_, xch, xf, lnb, lo) in two:
                    p.op("vector", lambda e, v_=c6(sb_, 4): e.tensor_scalar(
                        v_, v_, -1.0, LN_EPS, op0=ALU.mult, op1=ALU.add),
                         s_v if cc == cbk else None, 1)
                v_var2 = p.val(s_v)
                p.wait("scalar", s_v, v_var2)
                for (cc, sb_, xch, xf, lnb, lo) in two:
                    p.op("scalar", lambda e, v_=c6(sb_, 4), o=c6(sb_, 5): e.activation(
                        o, v_, AF.Sqrt), s_s if cc == cbk else None, 1)
                v_sd2 = p.val(s_s)
                p.wait("vector", s_s, v_sd2)
                for (cc, sb_, xch, xf, lnb, lo) in two:
                    p.op("vector", lambda e, s=c6(sb_, 5), o=col(ST_r2, cc): e.reciprocal(o, s))
                for (cc, sb_, xch, xf, lnb, lo) in two:
                    p.op("vector", lambda e, m=c6(sb_, 2), r=col(ST_r2, cc), o=col(ST_b2, cc):
                         e.scalar_tensor_tensor(o, m, -1.0, r, op0=ALU.mult, op1=ALU.mult),
                         s_v if cc == cbk else None, 1)
                v_st2 = p.val(s_v)
                p.wait("scalar", s_v, v_st2)
                for (cc, sb_, xch, xf, lnb, lo) in two:
                    p.op("scalar", lambda e, o=lnb, x_=xf, c=cc: e.activation(
                        o, x_, AF.Identity, bias=col(ST_b2, c), scale=col(ST_r2, c)), s_s, 1)
                v_ln2 = p.val(s_s)

                for (cc, sb_, xch, xf, lnb, lo) in two:
                    if tags.get("ps01") is not None:
                        p.wait("tensor", *tags["ps01"])
                    p.wait("tensor", s_s, v_xn)
                    for k in range(8):
                        bk2 = PS[k // 4]
                        p.op("tensor", lambda e, s=xf[:, k * 128:(k + 1) * 128],
                             d=bk2[:, (k % 4) * 128:(k % 4 + 1) * 128]: e.transpose(d, s, IDF[:]),
                             s_t if k == 7 else None, 1)
                    v_txn = p.val(s_t)
                    if tags.get("ps2") is not None:
                        p.wait("tensor", *tags["ps2"])
                    p.wait("tensor", s_s, v_ln2)
                    for k in range(8):
                        d = PS[2][:].bitcast(BF16)[:, k * 128:(k + 1) * 128]
                        p.op("tensor", lambda e, s=lnb[:, k * 128:(k + 1) * 128], d=d:
                             e.transpose(d, s, IDB[:]), s_t if k == 7 else None, 1)
                    v_tln = p.val(s_t)

                    xnT_dst = AR1[:].rearrange("q (dc c) -> q dc c", c=2048)
                    p.wait("vector", s_t, v_txn)
                    for hb in range(2):
                        dst = xnT_dst[:, hb * 4:(hb + 1) * 4, cc * 128:(cc + 1) * 128]
                        src2 = PS[hb][:].rearrange("q (k c) -> q k c", c=128)
                        p.op("vector", lambda e, d=dst, s=src2: e.tensor_copy(d, s))
                    p.drain("vector")
                    if lot_tag[cc % 2] is not None:
                        p.wait("vector", *lot_tag[cc % 2])
                    for hb in range(2):
                        xn_src = xnT_dst[:, hb * 4:(hb + 1) * 4, cc * 128:(cc + 1) * 128]
                        lod = lo.rearrange("q (k c) -> q k c", c=128)[:, hb * 4:(hb + 1) * 4, :]
                        src2 = PS[hb][:].rearrange("q (k c) -> q k c", c=128)
                        p.op("vector", lambda e, d=lod, s=src2, x_=xn_src: e.tensor_tensor(
                            d, s, x_, op=ALU.subtract), s_v if hb == 1 else None, 1)
                    v_exn = p.val(s_v)
                    tags["ps01"] = (s_v, v_exn)
                    tags[f"chxf{cc % 2}"] = (s_v, v_exn)

                    ln2T_dst = AR2[:].rearrange("q (dc c) -> q dc c", c=2048)
                    p.wait("vector", s_t, v_tln)
                    p.op("vector", lambda e, d=ln2T_dst[:, :, cc * 128:(cc + 1) * 128],
                         s=PS[2][:].bitcast(BF16).rearrange("q (k c) -> q k c", c=128): e.tensor_copy(d, s),
                         s_v, 1)
                    v_eln = p.val(s_v)
                    tags["ps2"] = (s_v, v_eln)
                    tags[f"chln{cc % 2}"] = (s_v, v_eln)

                    p.wait("tensor", s_v, v_exn)
                    rbank = PS[3] if cc % 2 == 0 else PS[6]
                    if tags.get(f"rbank{cc % 2}") is not None:
                        p.wait("tensor", *tags[f"rbank{cc % 2}"])
                    first = True
                    ROUT_TERMS = [(True, True), (True, False), (False, True), (False, False)]
                    for dc in range(8):
                        for lhs_hi, rhs_hi in ROUT_TERMS:
                            lhs = (AR1[:, dc * 2048 + cc * 128: dc * 2048 + (cc + 1) * 128]
                                   if lhs_hi else lo[:, dc * 128:(dc + 1) * 128])
                            rhs = (WRH if rhs_hi else WRL)[:, dc * 8:(dc + 1) * 8]
                            is_last = (dc == 7 and (lhs_hi, rhs_hi) == ROUT_TERMS[-1])
                            p.op("tensor", lambda e, l=lhs, r=rhs, f=first, z=is_last, pb=rbank: e.matmul(
                                pb[:, 0:8], l, r, start=f, stop=z),
                                s_t if is_last else None, 1)
                            first = False
                    v_rmm = p.val(s_t)
                    lot_tag[cc % 2] = (s_t, v_rmm)
                    p.wait("vector", s_t, v_rmm)
                    p.op("vector", lambda e, c=cc, pb=rbank: e.tensor_copy(
                        RZ[:, c * 8:(c + 1) * 8], pb[:, 0:8]), s_v, 1)
                    tags[f"rbank{cc % 2}"] = (s_v, p.val(s_v))
            p.wait("scalar", s_v, p.val(s_v))
            if debug_taps and b == 0:
                p.wait("sync", s_v, p.val(s_v))
                p.op("sync", lambda e: e.dma_start(out=taps["ps6"], in_=RZ[:]), s_o, 16)
                p.op("sync", lambda e: e.dma_start(out=taps["rt"], in_=RT[:]), s_o, 16)
            p.op("scalar", lambda e: e.activation(inh, RZ[:], AF.Sigmoid, scale=RT[:, 0:1]), s_s, 1)

            if debug_taps and b == 0:
                p.wait("sync", s_t, v_rmm)
                p.op("sync", lambda e: e.dma_start(out=taps["lot"], in_=LOT[1].bitcast(F32)), s_o, 16)

            def mm_group(eng_waits, bank_ap, pairs, n_free):
                for (sm, vv) in eng_waits:
                    p.wait("tensor", sm, vv)
                n = len(pairs)
                for i, (l, r) in enumerate(pairs):
                    p.op("tensor",
                         lambda e, l=l, r=r, f=(i == 0), z=(i == n - 1),
                         o=bank_ap[:, 0:n_free]: e.matmul(o, l, r, start=f, stop=z),
                         s_t if i == n - 1 else None, 1)
                return p.val(s_t)

            ln2s = AR2
            WBv = [WB[i].rearrange("p (dc f) -> p dc f", f=1024) for i in range(2)]
            ROW_ONES128 = ROWS[0:1, 3072:3200]
            bank_tag = {}

            def bwait(key):
                return [bank_tag[key]] if key in bank_tag else []

            if tags.get("ar6_free") is not None:
                p.wait("gpsimd", *tags["ar6_free"])
            v_wv = p.op("gpsimd", lambda e: e.dma_start(
                out=WBv[0], in_=qkvT_r[:, :, 2048:3072]), s_wa, 16)
            if tags.get("ar6_free") is not None:
                p.wait("gpsimd", *tags["ar6_free"])
            v_wq = p.op("gpsimd", lambda e: e.dma_start(
                out=WBv[1], in_=qkvT_r[:, :, 0:1024]), s_wb, 16)

            def bias_section(wb_idx, sec, scale):
                for fh in range(2):
                    pairs = [(NBTB[:, dc:dc + 1],
                              WB[wb_idx][:, dc * 1024 + fh * 512: dc * 1024 + (fh + 1) * 512])
                             for dc in range(8)]
                    pairs.append((ROW_ONE64,
                                  ROW_QKVB[:, sec * 1024 + fh * 512: sec * 1024 + (fh + 1) * 512]))
                    vv = mm_group(bwait("bias"), PS[6][0:1, :], pairs, 512)
                    p.wait("vector", s_t, vv)
                    dst = ROW_BIAS[:, sec * 1024 + fh * 512: sec * 1024 + (fh + 1) * 512]
                    if scale is None:
                        p.op("vector", lambda e, d=dst: e.tensor_copy(d, PS[6][0:1, 0:512]), s_v, 1)
                    else:
                        p.op("vector", lambda e, d=dst, sc=scale: e.tensor_scalar(
                            d, PS[6][0:1, 0:512], sc, None, op0=ALU.mult), s_v, 1)
                    bank_tag["bias"] = (s_v, p.val(s_v))
                return vv

            def fold(wb_idx, nwt_tile, wait_sem, wait_v, bias_v=None):
                p.wait("vector", wait_sem, wait_v)
                if bias_v is not None:
                    p.wait("vector", s_t, bias_v)
                for dc in range(8):
                    sl = WB[wb_idx][:, dc * 1024:(dc + 1) * 1024]
                    p.op("vector", lambda e, s=sl, c=nwt_tile[:, dc:dc + 1]: e.tensor_scalar(
                        s, s, c, None, op0=ALU.mult), s_v if dc == 7 else None, 1)
                return p.val(s_v)

            bias_v_v = bias_v_q = None
            if b == 0:
                bwv = [tags["rbank1"]] if "rbank1" in tags else []
                for w in bwv:
                    p.wait("tensor", w[0], w[1])
                p.wait("tensor", s_v, v_pro_done)
                p.wait("tensor", s_wa, v_wv)
                bias_v_v = bias_section(0, 2, None)
                p.wait("tensor", s_wb, v_wq)
                bias_v_q = bias_section(1, 0, ATT_SCALE)
            v_fv = fold(0, NWT, s_wa, v_wv, bias_v_v)
            v_fq = fold(1, NWTQ, s_wb, v_wq, bias_v_q)

            inh = RS[:, 0:128]
            dist = RS[:, 128:256]
            wz = RS[:, 256:384]
            fb = RS[:, 384:512]
            t1 = RS[:, 512:640]
            t2 = RS[:, 640:768]

            def r3(t):
                return t.rearrange("q (a e) -> q a e", e=8)

            def rb(t16):
                return t16.unsqueeze(2).broadcast_to([128, 16, 8])

            p.wait("scalar", s_v, p.val(s_v))
            if debug_taps and b == 0:
                p.wait("sync", s_v, p.val(s_v))
                p.op("sync", lambda e: e.dma_start(out=taps["ps6"], in_=RZ[:]), s_o, 16)
                p.op("sync", lambda e: e.dma_start(out=taps["rt"], in_=RT[:]), s_o, 16)
            p.op("scalar", lambda e: e.activation(inh, RZ[:], AF.Sigmoid, scale=RT[:, 0:1]), s_s, 1)
            v_inh = p.val(s_s)
            p.wait("vector", s_s, v_inh)
            p.op("vector", lambda e: e.tensor_scalar(
                dist, inh, -G_CENTER, None, op0=ALU.add))
            p.drain("vector")
            p.op("vector", lambda e: e.tensor_scalar(
                t1, dist, -1.0, None, op0=ALU.mult))
            p.drain("vector")
            p.op("vector", lambda e: e.tensor_tensor(dist, dist, t1, op=ALU.max), s_v, 1)
            p.drain("vector")
            p.wait("scalar", s_v, p.val(s_v))
            p.op("scalar", lambda e: e.activation(wz, dist, AF.Exp, scale=-10.0))
            p.op("scalar", lambda e: e.activation(fb, dist, AF.Exp, scale=-10.0 / 3.0), s_s, 1)
            v_exps = p.val(s_s)
            p.wait("vector", s_s, v_exps)
            p.op("vector", lambda e: e.tensor_scalar(t1, inh, G_LOWER, None, op0=ALU.is_ge))
            p.drain("vector")
            p.op("vector", lambda e: e.tensor_scalar(t2, inh, G_UPPER, None, op0=ALU.is_le))
            p.drain("vector")
            p.op("vector", lambda e: e.tensor_tensor(t1, t1, t2, op=ALU.mult))
            p.drain("vector")
            p.op("vector", lambda e: e.tensor_tensor(wz, wz, t1, op=ALU.mult))
            p.drain("vector")
            p.op("vector", lambda e: e.reduce_sum(t2[:, 0:16], r3(wz), axis=AX.X))
            p.drain("vector")
            p.op("vector", lambda e: e.reduce_max(t2[:, 16:32], r3(fb), axis=AX.X))
            p.drain("vector")
            p.op("vector", lambda e: e.tensor_tensor(t1, fb, rb(t2[:, 16:32]), op=ALU.is_ge))
            p.drain("vector")
            p.op("vector", lambda e: e.scalar_tensor_tensor(dist, t1, -1e30, fb, op0=ALU.mult, op1=ALU.add))
            p.drain("vector")
            p.op("vector", lambda e: e.reduce_max(t2[:, 32:48], r3(dist), axis=AX.X))
            p.drain("vector")
            p.op("vector", lambda e: e.tensor_tensor(dist, dist, rb(t2[:, 32:48]), op=ALU.is_ge))
            p.drain("vector")
            p.op("vector", lambda e: e.tensor_tensor(t1, t1, dist, op=ALU.add))
            p.drain("vector")
            p.op("vector", lambda e: e.tensor_tensor(fb, fb, t1, op=ALU.mult))
            p.drain("vector")
            p.op("vector", lambda e: e.reduce_sum(t2[:, 48:64], r3(fb), axis=AX.X))
            p.drain("vector")
            p.op("vector", lambda e: e.tensor_scalar(t2[:, 48:64], t2[:, 48:64], 1e-8, None, op0=ALU.max))
            p.drain("vector")
            p.op("vector", lambda e: e.reciprocal(t2[:, 64:80], t2[:, 48:64]))
            p.drain("vector")
            p.op("vector", lambda e: e.tensor_tensor(fb, fb, rb(t2[:, 64:80]), op=ALU.mult))
            p.drain("vector")
            p.op("vector", lambda e: e.tensor_scalar(t2[:, 80:96], t2[:, 0:16], 1e-8, None, op0=ALU.is_lt))
            p.drain("vector")
            p.op("vector", lambda e: e.tensor_tensor(fb, fb, wz, op=ALU.subtract))
            p.drain("vector")
            p.op("vector", lambda e: e.tensor_tensor(fb, fb, rb(t2[:, 80:96]), op=ALU.mult))
            p.drain("vector")
            p.op("vector", lambda e: e.tensor_tensor(wz, wz, fb, op=ALU.add))
            p.drain("vector")
            p.op("vector", lambda e: e.reduce_sum(t2[:, 96:112], r3(wz), axis=AX.X))
            p.drain("vector")
            p.op("vector", lambda e: e.tensor_scalar(t2[:, 96:112], t2[:, 96:112], 1e-8, None, op0=ALU.max))
            p.drain("vector")
            p.op("vector", lambda e: e.reciprocal(t2[:, 112:128], t2[:, 96:112]))
            p.drain("vector")
            p.op("vector", lambda e: e.tensor_tensor(RW[:], wz, rb(t2[:, 112:128]), op=ALU.mult), s_v, 1)
            p.drain("vector")
            v_rout = p.val(s_v)

            if debug_taps == "A" and b == 0:
                p.wait("sync", s_v, p.val(s_v))
                p.wait("sync", s_s, p.val(s_s))
                p.op("sync", lambda e: e.dma_start(out=taps["xnT"], in_=AR1[:].bitcast(F32)), s_o, 16)
                p.op("sync", lambda e: e.dma_start(out=taps["ln2T"], in_=AR2[:].bitcast(F32)), s_o, 16)
                p.op("sync", lambda e: e.dma_start(out=taps["rout"], in_=RW[:]), s_o, 16)
                p.op("sync", lambda e: e.dma_start(out=taps["stats"], in_=ST_r[:]), s_o, 16)
                p.op("sync", lambda e: e.dma_start(out=taps["sumd"], in_=SUMD[:]), s_o, 16)
                p.op("sync", lambda e: e.dma_start(out=taps["inh"], in_=RS[:, 0:128]), s_o, 16)
                p.op("sync", lambda e: e.dma_start(out=taps["sqd"], in_=SQD[:]), s_o, 16)
                p.wait("sync", s_o, p.val(s_o))
                break

            for fk in range(8):
                for cb in range(4):
                    pairs = [(WB[0][:, dc * 1024 + fk * 128: dc * 1024 + (fk + 1) * 128],
                              ln2s[:, dc * 2048 + cb * 512: dc * 2048 + (cb + 1) * 512])
                             for dc in range(8)]
                    pairs.append((ROW_BIAS[:, 2048 + fk * 128: 2048 + (fk + 1) * 128],
                                  ROW_ONES))
                    bk = f"v{(fk * 4 + cb) % 2}"
                    waits = bwait(bk) + ([(s_v, v_fv)] if fk == 0 and cb == 0 else [])
                    vv = mm_group(waits, PS[(fk * 4 + cb) % 2], pairs, 512)
                    p.wait("scalar", s_t, vv)
                    if b > 0 and tags.get("ar3_free_s") is not None and fk == 0 and cb == 0:
                        p.wait("scalar", *tags["ar3_free_s"])
                    dst = AR3[:, fk * 2048 + cb * 512: fk * 2048 + (cb + 1) * 512]
                    p.op("scalar", lambda e, d=dst, pb=PS[(fk * 4 + cb) % 2]: e.activation(
                        d, pb[:, 0:512], AF.Identity), s_s, 1)
                    bank_tag[bk] = (s_s, p.val(s_s))
                    v_v_last = vv

            for cc in range(TC):
                for fj in range(2):
                    pairs = [(ln2s[:, dc * 2048 + cc * 128: dc * 2048 + (cc + 1) * 128],
                              WB[1][:, dc * 1024 + fj * 512: dc * 1024 + (fj + 1) * 512])
                             for dc in range(8)]
                    pairs.append((ROW_ONES128,
                                  ROW_BIAS[:, fj * 512:(fj + 1) * 512]))
                    bk = f"q{(cc * 2 + fj) % 2}"
                    waits = bwait(bk) + ([(s_v, v_fq), tags.get("rbank0", (s_v, 0))]
                                         if cc == 0 and fj == 0 else [])
                    vv = mm_group(waits, PS[2 + (cc * 2 + fj) % 2], pairs, 512)
                    p.wait("scalar", s_t, vv)
                    dst = AR4[:, cc * 1024 + fj * 512: cc * 1024 + (fj + 1) * 512]
                    p.op("scalar", lambda e, d=dst, pb=PS[2 + (cc * 2 + fj) % 2]: e.activation(
                        d, pb[:, 0:512], AF.Identity), s_s, 1)
                    bank_tag[bk] = (s_s, p.val(s_s))

            p.wait("gpsimd", s_t, v_v_last)
            v_wk = p.op("gpsimd", lambda e: e.dma_start(
                out=WBv[0], in_=qkvT_r[:, :, 1024:2048]), s_wa, 16)
            bias_v_k = None
            if b == 0:
                p.wait("tensor", s_wa, v_wk)
                bias_v_k = bias_section(0, 1, None)
            v_fk = fold(0, NWT, s_wa, v_wk, bias_v_k)
            for cc in range(TC):
                for fj in range(2):
                    pairs = [(ln2s[:, dc * 2048 + cc * 128: dc * 2048 + (cc + 1) * 128],
                              WB[0][:, dc * 1024 + fj * 512: dc * 1024 + (fj + 1) * 512])
                             for dc in range(8)]
                    pairs.append((ROW_ONES128,
                                  ROW_BIAS[:, 1024 + fj * 512: 1024 + (fj + 1) * 512]))
                    bk = f"k{(cc * 2 + fj) % 2}"
                    waits = bwait(bk) + ([(s_v, v_fk)] if cc == 0 and fj == 0 else [])
                    vv = mm_group(waits, PS[4 + (cc * 2 + fj) % 2], pairs, 512)
                    p.wait("scalar", s_t, vv)
                    if b > 0 and cc == 0 and fj == 0 and tags.get("acc_out") is not None:
                        p.wait("scalar", *tags["acc_out"])
                    dst = AR5[:, cc * 1024 + fj * 512: cc * 1024 + (fj + 1) * 512]
                    p.op("scalar", lambda e, d=dst, pb=PS[4 + (cc * 2 + fj) % 2]: e.activation(
                        d, pb[:, 0:512], AF.Identity), s_s, 1)
                    bank_tag[bk] = (s_s, p.val(s_s))
            v_qkv_evacs = p.val(s_s)

            p.wait("tensor", s_s, v_qkv_evacs)
            v_s_last = {}
            for qi in range(8):
                for kh in range(2):
                    pairs = [(AR4[:, cc2 * 1024 + qi * 128: cc2 * 1024 + (qi + 1) * 128],
                              AR5[:, cc2 * 1024 + kh * 512: cc2 * 1024 + (kh + 1) * 512])
                             for cc2 in range(TC)]
                    bk = f"s{(qi % 2) * 2 + kh}"
                    vv = mm_group(bwait(bk), PS[(qi % 2) * 2 + kh], pairs, 512)
                    v_s_last[(qi, kh)] = vv
                pa = PS[(qi % 2) * 2]
                pb_ = PS[(qi % 2) * 2 + 1]
                p.wait("vector", s_t, v_s_last[(qi, 1)])
                p.op("vector", lambda e, q=qi, pb2=pa: e.reduce_max(
                    col(SMX, q * 2), pb2[:, 0:512], axis=AX.X))
                p.op("vector", lambda e, q=qi, pb2=pb_: e.reduce_max(
                    col(SMX, q * 2 + 1), pb2[:, 0:512], axis=AX.X))
                p.drain("vector")
                p.op("vector", lambda e, q=qi: e.tensor_tensor(
                    col(ST_t, 6), col(SMX, q * 2), col(SMX, q * 2 + 1), op=ALU.max))
                p.drain("vector")
                p.op("vector", lambda e, q=qi: e.tensor_scalar(
                    col(SNMX, q), col(ST_t, 6), -1.0, None, op0=ALU.mult), s_v, 1)
                v_nmx = p.val(s_v)
                p.wait("scalar", s_v, v_nmx)
                if b > 0 and qi == 0 and tags.get("acc_out") is not None:
                    p.wait("scalar", *tags["acc_out"])
                for kh in range(2):
                    pbx = PS[(qi % 2) * 2 + kh]
                    dst = AR6f[:, qi * 1024 + kh * 512: qi * 1024 + (kh + 1) * 512]
                    p.op("scalar", lambda e, d=dst, pb2=pbx, q=qi, k2=kh: e.activation(
                        d, pb2[:, 0:512], AF.Exp, bias=col(SNMX, q),
                        accum_out=col(SSUM, q * 2 + k2)), s_s, 1)
                v_exp = p.val(s_s)
                bank_tag[f"s{(qi % 2) * 2}"] = (s_s, v_exp)
                bank_tag[f"s{(qi % 2) * 2 + 1}"] = (s_s, v_exp)
                p.wait("vector", s_s, v_exp)
                p.op("vector", lambda e, q=qi: e.tensor_tensor(
                    col(ST_t, 6), col(SSUM, q * 2), col(SSUM, q * 2 + 1), op=ALU.add))
                p.drain("vector")
                p.op("vector", lambda e, q=qi: e.reciprocal(col(SRCP, q), col(ST_t, 6)), s_v, 1)
            v_smax_done = p.val(s_v)
            v_s_groups_done = v_s_last[(7, 1)]

            AR2v = AR2[:].rearrange("pq (ki q) -> pq ki q", q=1024)
            v_exp_all = p.val(s_s)
            p.wait("tensor", s_s, v_exp_all)
            for qi in range(8):
                for kg in range(2):
                    bk = f"t{kg}"
                    for w in bwait(bk):
                        p.wait("tensor", w[0], w[1])
                    pbank = PS[4 + kg]
                    for j in range(4):
                        src = AR6f[:, qi * 1024 + (kg * 4 + j) * 128: qi * 1024 + (kg * 4 + j + 1) * 128]
                        p.op("tensor", lambda e, s2=src, d=pbank[:, j * 128:(j + 1) * 128]:
                             e.transpose(d, s2, IDF[:]), s_t if j == 3 else None, 1)
                    vv = p.val(s_t)
                    p.wait("vector", s_t, vv)
                    dst = AR2v[:, kg * 4:(kg + 1) * 4, qi * 128:(qi + 1) * 128]
                    src3 = pbank[:].rearrange("pq (k c2) -> pq k c2", c2=128)
                    p.op("vector", lambda e, d=dst, s2=src3: e.tensor_copy(d, s2), s_v, 1)
                    bank_tag[bk] = (s_v, p.val(s_v))
            v_swt_done = p.val(s_v)

            p.wait("gpsimd", s_t, v_s_groups_done)
            v_wp = p.op("gpsimd", lambda e: e.dma_start(
                out=WPJ.rearrange("pq (dc f) -> pq dc f", f=1024), in_=projT_r[:]), s_wp, 16)

            xr_tag = [None, None]
            xr_v = [0, 0]
            u_tag = [None, None]
            for cb in range(4):
                for qi in range(8):
                    pairs = [(AR2[:, ki * 1024 + qi * 128: ki * 1024 + (qi + 1) * 128],
                              AR3[:, ki * 2048 + cb * 512: ki * 2048 + (cb + 1) * 512])
                             for ki in range(8)]
                    bk = f"a{qi % 2}"
                    waits = bwait(bk)
                    if cb == 0 and qi == 0:
                        waits = waits + [(s_v, v_swt_done)]
                    if qi == 0 and cb >= 2:
                        pass
                    vv = mm_group(waits, PS[qi % 2], pairs, 512)
                    p.wait("scalar", s_t, vv)
                    p.wait("scalar", s_v, v_smax_done)
                    dst = ATB[cb % 2][:, qi * 512:(qi + 1) * 512]
                    p.op("scalar", lambda e, d=dst, pb2=PS[qi % 2], q=qi: e.activation(
                        d, pb2[:, 0:512], AF.Identity, scale=col(SRCP, q)), s_s, 1)
                    bank_tag[bk] = (s_s, p.val(s_s))
                    v_A_mm_last = vv
                v_atb = p.val(s_s)

                for cj in range(4):
                    cc2 = cb * 4 + cj
                    sl = cj % 2
                    if xr_tag[sl] is not None:
                        p.wait("sync", *xr_tag[sl])
                    if cb == 0 and cj < 2:
                        p.wait("sync", s_t, v_s_groups_done)
                    xr_v[sl] = p.op("sync", lambda e, d=XRS[sl], s2=x_r[b, :, cc2, :]:
                                    e.dma_start(out=d, in_=s2), s_xr[sl], 16)
                    p.wait("scalar", s_xr[sl], xr_v[sl])
                    if u_tag[sl] is not None:
                        p.wait("scalar", *u_tag[sl])
                    p.op("scalar", lambda e, d=URS[sl], s2=XRS[sl], c=cc2: e.activation(
                        d, s2, AF.Identity, bias=col(ST_b1, c), scale=col(ST_us, c)), s_s, 1)
                    v_u = p.val(s_s)
                    xr_tag[sl] = (s_s, v_u)

                    for dj in range(2):
                        pairs = [(ATB[cb % 2][:, qi * 512 + cj * 128: qi * 512 + cj * 128 + 128],
                                  WPJ[:, qi * 1024 + dj * 512: qi * 1024 + (dj + 1) * 512])
                                 for qi in range(8)]
                        pairs.append((ROW_ONES128,
                                      ROW_PROJB[:, dj * 512:(dj + 1) * 512]))
                        bk = f"p{(cj * 2 + dj) % 2}"
                        waits = bwait(bk) + [(s_s, v_atb)]
                        if cb == 0 and cj == 0 and dj == 0:
                            waits.append((s_wp, v_wp))
                        vv = mm_group(waits, PS[2 + (cj * 2 + dj) % 2], pairs, 512)
                        p.wait("vector", s_t, vv)
                        p.wait("vector", s_s, v_u)
                        if b > 0 and tags.get("acc_out") is not None and cb == 0 and cj == 0 and dj == 0:
                            p.wait("vector", *tags["acc_out"])
                        av = accv(cc2)
                        p.op("vector", lambda e, a=av[:, dj * 512:(dj + 1) * 512],
                             pb2=PS[2 + (cj * 2 + dj) % 2], us=URS[sl][:, dj * 512:(dj + 1) * 512]:
                             e.tensor_tensor(a, pb2[:, 0:512], us, op=ALU.add),
                             s_v, 1)
                        bank_tag[bk] = (s_v, p.val(s_v))
                    u_tag[sl] = (s_v, p.val(s_v))
            v_acc_init = p.val(s_v)

            v_last_A = v_A_mm_last
            if skip_ffn:
                v_ffn_done = v_acc_init
                v_gate_last = v_last_A
                v_eout_groups = v_last_A
            gwait = [(s_t, v_last_A)]
            for kk in ("g0", "g1", "u0", "u1", "e0", "e1", "e2", "e3"):
                bank_tag[kk] = (s_v, v_acc_init)
            v_wg_l = v_wu_l = v_wd_l = 0
            GSv = GS.rearrange("pq (dc f) -> pq dc f", f=1024)
            USv = US.rearrange("pq (dc f) -> pq dc f", f=1024)
            DSv = DS.rearrange("pq (dc f) -> pq dc f", f=1024)
            for e_i in range(E if not skip_ffn else 0):
                if e_i == 0:
                    for (sm, vv2) in gwait:
                        p.wait("gpsimd", sm, vv2)
                    v_wg_l = p.op("gpsimd", lambda e, i=e_i: e.dma_start(
                        out=GSv, in_=wg_r[i]), s_wg, 16)
                    v_wu_l = p.op("gpsimd", lambda e, i=e_i: e.dma_start(
                        out=USv, in_=wu_r[i]), s_wg, 16)
                    v_wd_l = p.op("gpsimd", lambda e, i=e_i: e.dma_start(
                        out=DSv, in_=wd_r[i]), s_wd, 16)
                v_gate_last = 0
                v_h_last = 0
                for cb in range(4):
                    for ic in range(8):
                        par = ic % 2
                        pairs_g = [(GS[:, dc * 1024 + ic * 128: dc * 1024 + (ic + 1) * 128],
                                    AR1[:, dc * 2048 + cb * 512: dc * 2048 + (cb + 1) * 512])
                                   for dc in range(8)]
                        pairs_u = [(US[:, dc * 1024 + ic * 128: dc * 1024 + (ic + 1) * 128],
                                    AR1[:, dc * 2048 + cb * 512: dc * 2048 + (cb + 1) * 512])
                                   for dc in range(8)]
                        waits = bwait(f"g{par}")
                        if e_i == 0 and cb == 0 and ic == 0:
                            waits = waits + [(s_wg, v_wu_l)]
                        vg = mm_group(waits, PS[par * 2], pairs_g, 512)
                        vu = mm_group(bwait(f"u{par}"), PS[par * 2 + 1], pairs_u, 512)
                        v_gate_last = vu
                        p.wait("scalar", s_t, vg)
                        if tags.get(f"silu{par}") is not None:
                            p.wait("scalar", *tags[f"silu{par}"])
                        p.op("scalar", lambda e, d=SILU[par], pb2=PS[par * 2]: e.activation(
                            d, pb2[:, 0:512], AF.Silu), s_s, 1)
                        v_si = p.val(s_s)
                        bank_tag[f"g{par}"] = (s_s, v_si)
                        p.wait("vector", s_s, v_si)
                        p.wait("vector", s_t, vu)
                        if tags.get(f"hslot{ic}") is not None:
                            p.wait("vector", *tags[f"hslot{ic}"])
                        p.op("vector", lambda e, d=HB[:, ic * 512:(ic + 1) * 512],
                             s2=SILU[par], pb2=PS[par * 2 + 1]: e.tensor_tensor(
                             d, s2, pb2[:, 0:512], op=ALU.mult), s_v, 1)
                        v_h = p.val(s_v)
                        v_h_last = v_h
                        bank_tag[f"u{par}"] = (s_v, v_h)
                        tags[f"silu{par}"] = (s_v, v_h)
                    for cj in range(4):
                        cc2 = cb * 4 + cj
                        for dj in range(2):
                            pairs = [(HB[:, ic2 * 512 + cj * 128: ic2 * 512 + cj * 128 + 128],
                                      DS[:, ic2 * 1024 + dj * 512: ic2 * 1024 + (dj + 1) * 512])
                                     for ic2 in range(8)]
                            bk = f"e{(cj * 2 + dj) % 4}"
                            waits = bwait(bk) + [(s_v, v_h_last)]
                            if cb == 0 and cj == 0 and dj == 0:
                                waits.append((s_wd, v_wd_l))
                            vv = mm_group(waits, PS[4 + (cj * 2 + dj) % 4], pairs, 512)
                            p.wait("vector", s_t, vv)
                            av = accv(cc2)
                            p.op("vector", lambda e, a=av[:, dj * 512:(dj + 1) * 512],
                                 pb2=PS[4 + (cj * 2 + dj) % 4], c=cc2, i=e_i:
                                 e.scalar_tensor_tensor(
                                     a, pb2[:, 0:512], RW[:, c * 8 + i: c * 8 + i + 1], a,
                                     op0=ALU.mult, op1=ALU.add), s_v, 1)
                            bank_tag[bk] = (s_v, p.val(s_v))
                    for ic2 in range(8):
                        tags[f"hslot{ic2}"] = (s_t, p.val(s_t))
                v_eout_groups = p.val(s_t)
                if e_i + 1 < E:
                    p.wait("gpsimd", s_t, v_gate_last)
                    v_wg_l = p.op("gpsimd", lambda e, i=e_i + 1: e.dma_start(
                        out=GSv, in_=wg_r[i]), s_wg, 16)
                    v_wu_l = p.op("gpsimd", lambda e, i=e_i + 1: e.dma_start(
                        out=USv, in_=wu_r[i]), s_wg, 16)
                    p.wait("gpsimd", s_t, v_eout_groups)
                    v_wd_l = p.op("gpsimd", lambda e, i=e_i + 1: e.dma_start(
                        out=DSv, in_=wd_r[i]), s_wd, 16)
                    bank_tag["g0"] = (s_wg, v_wu_l)
            if not skip_ffn:
                v_ffn_done = p.val(s_v)

            if b + 1 < nb:
                pre = []
                for sl in range(4):
                    if xslot_tag[sl] is not None:
                        p.wait("sync", *xslot_tag[sl])
                    p.wait("sync", s_v, v_acc_init)
                    pre.append(p.op(
                        "sync",
                        lambda e, d=XCH[sl], s=x_r[b + 1, :, sl, :]: e.dma_start(out=d, in_=s),
                        s_xc[sl], 16))
                tags["pre_x"] = pre

            p.wait("sync", s_v, v_ffn_done)
            p.op("sync", lambda e, bb=b: e.dma_start(
                out=out_r[bb, :, 0:8, :],
                in_=AR5f[:].rearrange("pq (n m) -> pq n m", m=1024)), s_o, 16)
            p.op("sync", lambda e, bb=b: e.dma_start(
                out=out_r[bb, :, 8:16, :],
                in_=AR6f[:].rearrange("pq (n m) -> pq n m", m=1024)), s_o, 16)
            v_out = p.val(s_o)
            tags["acc_out"] = (s_o, v_out)
            tags["ar6_free"] = (s_o, v_out)
            tags["ar4_free"] = (s_v, v_acc_init)
            tags["ar3_free_s"] = (s_t, v_gate_last)
            tags["ar1_free"] = (s_t, v_gate_last)
            tags["ar2_free"] = (s_t, v_eout_groups)

            if debug_taps == "B" and b == 0:
                p.wait("sync", s_o, p.val(s_o))
                break

        p.wait("sync", s_o, p.val(s_o))

        with nc.Block() as block:
            @block.sync
            def _(h):
                p.replay("sync", h)

            @block.gpsimd
            def _(h):
                p.replay("gpsimd", h)

            @block.tensor
            def _(h):
                p.replay("tensor", h)

            @block.vector
            def _(h):
                p.replay("vector", h)

            @block.scalar
            def _(h):
                p.replay("scalar", h)

    return nc


def host_inputs(inputs, core):
    x = np.asarray(inputs["L0_attn_x"])
    m = {
        "x": np.ascontiguousarray(x.reshape(N_CORES, NB, T, D)[core]),
        "qkvT": np.ascontiguousarray(np.asarray(inputs["L0_attn_qkv_w"]).T),
        "projT": np.ascontiguousarray(np.asarray(inputs["L0_attn_proj_w"]).T),
        "wg": np.ascontiguousarray(np.asarray(inputs["L0_ffn_Wg"])),
        "wu": np.ascontiguousarray(np.asarray(inputs["L0_ffn_Wu"])),
        "wd": np.ascontiguousarray(np.asarray(inputs["L0_ffn_Wd"])),
        "wrT": np.ascontiguousarray(np.asarray(inputs["L0_ffn_Wr"]).T),
        "normw": np.asarray(inputs["L0_attn_norm_w"]).reshape(1, D).copy(),
        "normb": np.asarray(inputs["L0_attn_norm_b"]).reshape(1, D).copy(),
        "qkvb": np.asarray(inputs["L0_attn_qkv_b"]).reshape(1, 3 * D).copy(),
        "projb": np.asarray(inputs["L0_attn_proj_b"]).reshape(1, D).copy(),
        "temp": np.asarray(inputs["L0_ffn_temperature"]).reshape(1, 1).copy(),
        "idenb": np.eye(128, dtype=ml_dtypes.bfloat16),
        "idenf": np.eye(128, dtype=np.float32),
        "onesb": np.ones((1, 512), dtype=ml_dtypes.bfloat16),
        "onesf": np.ones((1, 128), dtype=np.float32),
    }
    return m


def kernel(**inputs):
    nc = build_nc(NB, debug_taps=DEBUG_TAPS)
    in_maps = [host_inputs(inputs, c) for c in range(N_CORES)]
    res = run_bass_kernel_spmd(nc, in_maps, core_ids=list(range(N_CORES)))
    out = np.concatenate([np.asarray(r["out"]) for r in res.results], axis=0)
    return out.reshape(32, T, D)


# version 09
# speedup vs baseline: 1.0585x; 1.0585x over previous
import math
import contextlib
import numpy as np
import ml_dtypes

import concourse.bass as bass
import concourse.mybir as mybir
from concourse.bass_utils import run_bass_kernel_spmd

F32 = mybir.dt.float32
BF16 = mybir.dt.bfloat16
AF = mybir.ActivationFunctionType
ALU = mybir.AluOpType
AX = mybir.AxisListType

N_CORES = 8
NB = 4
T = 2048
D = 1024
DC = D // 128
TC = T // 128
E = 8
LN_EPS = 1e-5
ATT_SCALE = 1.0 / math.sqrt(T)
G_CENTER = 1.0 / math.e
G_LOWER = 0.5 - math.log(4.0 / 3.0)
G_UPPER = 0.5

DEBUG_TAPS = False

ENGINES = ("sync", "gpsimd", "tensor", "vector", "scalar")


class Plan:

    def __init__(self):
        self.stream = {e: [] for e in ENGINES}
        self.cnt = {}

    def val(self, sem):
        return self.cnt.get(id(sem), 0)

    def wait(self, eng, sem, v=None):
        if v is None:
            v = self.val(sem)
        if v <= 0:
            return
        self.stream[eng].append(("w", sem, v))

    def drain(self, eng):
        self.stream[eng].append(("o", lambda e: e.drain(), None, 1))

    def op(self, eng, fn, sem=None, n=1):
        v = None
        if sem is not None:
            v = self.val(sem) + n
            self.cnt[id(sem)] = v
        self.stream[eng].append(("o", fn, sem, n))
        return v

    def replay(self, eng_name, h):
        for it in self.stream[eng_name]:
            if it[0] == "w":
                h.wait_ge(it[1], it[2])
            else:
                ins = it[1](h)
                if it[2] is not None:
                    ins.then_inc(it[2], it[3])


def build_nc(nb=NB, debug_taps=False, skip_ffn=False):
    nc = bass.Bass("TRN2", target_bir_lowering=False)

    dx = nc.dram_tensor("x", [nb, T, D], F32, kind="ExternalInput").ap()
    dqkvT = nc.dram_tensor("qkvT", [D, 3 * D], F32, kind="ExternalInput").ap()
    dprojT = nc.dram_tensor("projT", [D, D], F32, kind="ExternalInput").ap()
    dwg = nc.dram_tensor("wg", [E, D, D], F32, kind="ExternalInput").ap()
    dwu = nc.dram_tensor("wu", [E, D, D], F32, kind="ExternalInput").ap()
    dwd = nc.dram_tensor("wd", [E, D, D], F32, kind="ExternalInput").ap()
    dwrT = nc.dram_tensor("wrT", [E, D], F32, kind="ExternalInput").ap()
    dnormw = nc.dram_tensor("normw", [1, D], F32, kind="ExternalInput").ap()
    dnormb = nc.dram_tensor("normb", [1, D], F32, kind="ExternalInput").ap()
    dqkvb = nc.dram_tensor("qkvb", [1, 3 * D], F32, kind="ExternalInput").ap()
    dprojb = nc.dram_tensor("projb", [1, D], F32, kind="ExternalInput").ap()
    dtemp = nc.dram_tensor("temp", [1, 1], F32, kind="ExternalInput").ap()
    didenb = nc.dram_tensor("idenb", [128, 128], BF16, kind="ExternalInput").ap()
    didenf = nc.dram_tensor("idenf", [128, 128], F32, kind="ExternalInput").ap()
    donesb = nc.dram_tensor("onesb", [1, 512], BF16, kind="ExternalInput").ap()
    donesf = nc.dram_tensor("onesf", [1, 128], F32, kind="ExternalInput").ap()
    dout = nc.dram_tensor("out", [nb, T, D], F32, kind="ExternalOutput").ap()

    taps = {}
    if debug_taps:
        taps["xnT"] = nc.dram_tensor("t_xnT", [128, 8192], F32, kind="ExternalOutput").ap()
        taps["ln2T"] = nc.dram_tensor("t_ln2T", [128, 8192], F32, kind="ExternalOutput").ap()
        taps["rout"] = nc.dram_tensor("t_rout", [128, 128], F32, kind="ExternalOutput").ap()
        taps["stats"] = nc.dram_tensor("t_stats", [128, 16], F32, kind="ExternalOutput").ap()
        taps["sumd"] = nc.dram_tensor("t_sumd", [128, 16], F32, kind="ExternalOutput").ap()
        taps["sqd"] = nc.dram_tensor("t_sqd", [128, 16], F32, kind="ExternalOutput").ap()
        taps["inh"] = nc.dram_tensor("t_inh", [128, 128], F32, kind="ExternalOutput").ap()
        taps["ps6"] = nc.dram_tensor("t_ps6", [128, 128], F32, kind="ExternalOutput").ap()
        taps["rt"] = nc.dram_tensor("t_rt", [128, 1], F32, kind="ExternalOutput").ap()
        taps["lot"] = nc.dram_tensor("t_lot", [128, 512], F32, kind="ExternalOutput").ap()
        taps["wrf"] = nc.dram_tensor("t_wrf", [128, 64], F32, kind="ExternalOutput").ap()
        taps["wrhl"] = nc.dram_tensor("t_wrhl", [128, 32], F32, kind="ExternalOutput").ap()
        taps["nwt"] = nc.dram_tensor("t_nwt", [128, 8], F32, kind="ExternalOutput").ap()
        taps["z0"] = nc.dram_tensor("t_z0", [128, 8], F32, kind="ExternalOutput").ap()

    with contextlib.ExitStack() as ctx:
        sb = lambda name, shape, dt: ctx.enter_context(nc.sbuf_tensor(name, shape, dt))
        ps = lambda name: ctx.enter_context(nc.psum_tensor(name, [128, 512], F32))
        sem = lambda name: ctx.enter_context(nc.semaphore(name))

        AR1 = sb("ar1", [128, 16384], BF16)
        AR2 = sb("ar2", [128, 16384], BF16)
        AR3 = sb("ar3", [128, 16384], BF16)
        AR4 = sb("ar4", [128, 16384], BF16)
        AR5 = sb("ar5", [128, 16384], BF16)
        AR6 = sb("ar6", [128, 16384], BF16)

        AR4f = AR4[:].bitcast(F32)
        AR5f = AR5[:].bitcast(F32)
        AR6f = AR6[:].bitcast(F32)

        IDB = sb("idb", [128, 128], BF16)
        IDF = sb("idf", [128, 128], F32)
        ROWS = sb("rows", [128, 4608], BF16)
        NWT = sb("nwt", [128, 8], F32)
        NWTQ = sb("nwtq", [128, 8], F32)
        NBTB = sb("nbtb", [128, 8], BF16)
        WRH = sb("wrh", [128, 64], BF16)
        WRL = sb("wrl", [128, 64], BF16)
        WRF = sb("wrf", [128, 64], F32)
        RT = sb("rt", [128, 1], F32)
        ST_r = sb("st_r", [128, 16], F32)
        ST_b1 = sb("st_b1", [128, 16], F32)
        ST_us = sb("st_us", [128, 16], F32)
        ST_r2 = sb("st_r2", [128, 16], F32)
        ST_b2 = sb("st_b2", [128, 16], F32)
        ST_t = sb("st_t", [128, 16], F32)
        SMX = sb("smx", [128, 16], F32)
        SSUM = sb("ssum", [128, 16], F32)
        SNMX = sb("snmx", [128, 8], F32)
        SRCP = sb("srcp", [128, 8], F32)
        SUMD = sb("sumd", [128, 16], F32)
        SQD = sb("sqd", [128, 16], F32)
        RW = sb("rw", [128, 128], F32)
        RZ = sb("rz", [128, 128], F32)
        RS = sb("rs", [128, 768], F32)

        PS = [ps(f"ps{i}") for i in range(8)]

        s_x = sem("s_x")
        s_xc = [sem(f"s_xc{i}") for i in range(4)]
        s_wa = sem("s_wa")
        s_wb = sem("s_wb")
        s_wg = sem("s_wg")
        s_wd = sem("s_wd")
        s_wp = sem("s_wp")
        s_xr = [sem("s_xr0"), sem("s_xr1")]
        s_t = sem("s_t")
        s_v = sem("s_v")
        s_s = sem("s_s")
        s_o = sem("s_o")

        p = Plan()

        def col(tile, c):
            return tile[:, c:c + 1]

        x_r = dx.rearrange("b (n p) m -> b p n m", p=128)
        out_r = dout.rearrange("b (n p) m -> b p n m", p=128)
        qkvT_r = dqkvT.rearrange("(dc p) f -> p dc f", p=128)
        projT_r = dprojT.rearrange("(dc p) f -> p dc f", p=128)
        wg_r = dwg.rearrange("e (dc p) i -> e p dc i", p=128)
        wu_r = dwu.rearrange("e (dc p) i -> e p dc i", p=128)
        wd_r = dwd.rearrange("e (ic p) d -> e p ic d", p=128)

        def ar3v(c0, c1):
            return AR3[:, c0:c1]

        XCH = [AR4[:, 8192 + i * 2048: 8192 + (i + 1) * 2048].bitcast(F32) for i in range(4)]
        CHXF = [AR4[:, i * 2048: (i + 1) * 2048].bitcast(F32) for i in range(2)]
        CHLN = [AR4[:, 4096 + i * 1024: 4096 + (i + 1) * 1024] for i in range(2)]
        LOT = [AR4[:, 6144 + i * 1024: 6144 + (i + 1) * 1024] for i in range(2)]

        def swT(ki, qi):
            return AR2[:, ki * 1024 + qi * 128: ki * 1024 + (qi + 1) * 128]

        ATB = [AR2[:, 8192 + i * 4096: 8192 + (i + 1) * 4096] for i in range(2)]

        GS = AR3[:, 0:8192]
        US = AR3[:, 8192:16384]
        DS = AR2[:, 0:8192]
        HB = [AR2[:, 8192 + i * 4096: 8192 + (i + 1) * 4096] for i in range(2)]

        WPJ = AR4[:, 0:8192]
        XRS = [AR4[:, 8192 + i * 2048: 8192 + (i + 1) * 2048].bitcast(F32) for i in range(2)]
        URS = [AR4[:, 12288 + i * 2048: 12288 + (i + 1) * 2048].bitcast(F32) for i in range(2)]

        WB = [AR6[:, 0:8192], AR6[:, 8192:16384]]

        def accv(cc):
            a = AR5f if cc < 8 else AR6f
            return a[:, (cc % 8) * 1024: (cc % 8 + 1) * 1024]

        ROW_BIAS = ROWS[0:1, 0:3072]
        ROW_ONES = ROWS[0:1, 3072:3584]
        ROW_PROJB = ROWS[0:1, 3584:4608]
        ROW_ONE64 = ROWS[64:65, 0:1]
        ROW_QKVB = ROWS[64:65, 1:3073]

        stage = AR4f
        st_wr = stage[0:8, 0:1024]
        st_nw = stage[0:1, 1024:2048]
        st_nb = stage[0:1, 2048:3072]
        st_pb = stage[0:1, 3072:4096]
        st_qb = stage[0:1, 4096:7168]
        st_tp = stage[0:1, 7168:7169]
        st_rc = stage[0:1, 7169:7170]
        st_of = stage[0:1, 7296:7424]
        st_wh = stage[1:2, 0:64]
        st_whf = stage[0:1, 7424:7488]

        p.op("sync", lambda e: e.dma_start(out=IDB[:], in_=didenb[:]), s_x, 16)
        p.op("sync", lambda e: e.dma_start(out=IDF[:], in_=didenf[:]), s_x, 16)
        p.op("sync", lambda e: e.dma_start(out=st_wr, in_=dwrT[:]), s_x, 16)
        p.op("sync", lambda e: e.dma_start(out=st_nw, in_=dnormw[:]), s_x, 16)
        p.op("sync", lambda e: e.dma_start(out=st_nb, in_=dnormb[:]), s_x, 16)
        p.op("sync", lambda e: e.dma_start(out=st_pb, in_=dprojb[:]), s_x, 16)
        p.op("sync", lambda e: e.dma_start(out=st_qb, in_=dqkvb[:]), s_x, 16)
        p.op("sync", lambda e: e.dma_start(out=st_tp, in_=dtemp[:]), s_x, 16)
        p.op("sync", lambda e: e.dma_start(out=ROW_ONES, in_=donesb[:]), s_x, 16)
        p.op("sync", lambda e: e.dma_start(out=ROW_ONE64, in_=donesb[0:1, 0:1]), s_x, 16)
        p.op("sync", lambda e: e.dma_start(out=st_of, in_=donesf[:]), s_x, 16)
        v_pro_loads = p.val(s_x)

        p.wait("tensor", s_x, v_pro_loads)
        for dc in range(8):
            src = st_nw[:, dc * 128:(dc + 1) * 128]
            p.op("tensor", lambda e, s=src, d=PS[0][:, dc:dc + 1]: e.transpose(d, s, IDF[0:1, 0:1]))
        for dc in range(8):
            src = st_nb[:, dc * 128:(dc + 1) * 128]
            p.op("tensor", lambda e, s=src, d=PS[0][:, 8 + dc:9 + dc]: e.transpose(d, s, IDF[0:1, 0:1]))
        for dc in range(8):
            src = st_wr[:, dc * 128:(dc + 1) * 128]
            p.op("tensor", lambda e, s=src, d=PS[1][:, dc * 8:(dc + 1) * 8]: e.transpose(d, s, IDF[0:8, 0:8]))
        p.wait("vector", s_x, v_pro_loads)
        p.op("vector", lambda e: e.reciprocal(st_rc, st_tp), s_v, 1)
        v_rc = p.val(s_v)
        p.wait("tensor", s_v, v_rc)
        vt_pro = p.op("tensor", lambda e: e.matmul(PS[2][:, 0:1], st_of, st_rc, start=True, stop=True), s_t, 1)

        p.wait("vector", s_t, vt_pro)
        p.op("vector", lambda e: e.tensor_copy(NWT[:], PS[0][:, 0:8]))
        p.op("vector", lambda e: e.tensor_scalar(NWTQ[:], PS[0][:, 0:8], ATT_SCALE, None, op0=ALU.mult))
        p.op("vector", lambda e: e.tensor_copy(NBTB[:], PS[0][:, 8:16]))
        p.op("vector", lambda e: e.tensor_copy(WRF[:], PS[1][:, 0:64]))
        p.op("vector", lambda e: e.tensor_copy(RT[:], PS[2][:, 0:1]))
        p.drain("vector")
        p.op("vector", lambda e: e.tensor_copy(WRH[:], WRF[:]))
        p.drain("vector")
        p.op("vector", lambda e: e.tensor_copy(RS[:, 0:64], WRH[:]))
        p.drain("vector")
        p.op("vector", lambda e: e.tensor_tensor(WRL[:], WRF[:], RS[:, 0:64], op=ALU.subtract))
        p.op("vector", lambda e: e.tensor_copy(ROW_PROJB, st_pb))
        p.op("vector", lambda e: e.tensor_copy(ROW_QKVB, st_qb), s_v, 1)
        v_pro_done = p.val(s_v)
        if debug_taps:
            p.wait("sync", s_v, v_pro_done)
            p.op("sync", lambda e: e.dma_start(out=taps["wrf"], in_=WRF[:]), s_o, 16)
            p.op("sync", lambda e: e.dma_start(out=taps["wrhl"], in_=WRH[:].bitcast(F32)), s_o, 16)
            p.op("sync", lambda e: e.dma_start(out=taps["nwt"], in_=NWT[:]), s_o, 16)

        tags = {}

        for b in range(nb):
            xslot_tag = [None] * 4
            lot_tag = [None, None]
            xdma_v = [0] * TC

            def emit_xdma(cc):
                sl = cc % 4
                if xslot_tag[sl] is not None:
                    p.wait("sync", *xslot_tag[sl])
                if cc == 0 and b == 0:
                    p.wait("sync", s_v, v_pro_done)
                if cc == 0 and b > 0 and tags.get("ar4_free") is not None:
                    p.wait("sync", *tags["ar4_free"])
                xdma_v[cc] = p.op(
                    "sync",
                    lambda e, d=XCH[sl], s=x_r[b, :, cc, :]: e.dma_start(out=d, in_=s),
                    s_xc[sl], 16)

            pre_x = tags.pop("pre_x", None)
            if pre_x is not None:
                for i4 in range(4):
                    xdma_v[i4] = pre_x[i4]
            else:
                emit_xdma(0)
                emit_xdma(1)
                emit_xdma(2)
                emit_xdma(3)
            for pp in range(TC // 2):
                ca, cbk = 2 * pp, 2 * pp + 1
                if cbk + 2 < TC and pp >= 1:
                    emit_xdma(cbk + 1)
                    emit_xdma(cbk + 2)
                two = ((ca, 0, XCH[ca % 4], CHXF[0], CHLN[0], LOT[0]),
                       (cbk, 8, XCH[cbk % 4], CHXF[1], CHLN[1], LOT[1]))

                def c6(base, j):
                    return ST_t[:, base + j: base + j + 1]

                for (cc, sb_, xch, xf, lnb, lo) in two:
                    p.wait("vector", s_xc[cc % 4], xdma_v[cc])
                    p.op("vector", lambda e, x_=xch, o=c6(sb_, 0): e.reduce_sum(o, x_, axis=AX.X))
                for (cc, sb_, xch, xf, lnb, lo) in two:
                    p.wait("scalar", s_xc[cc % 4], xdma_v[cc])
                    if tags.get(f"chxf{cc % 2}") is not None:
                        p.wait("scalar", *tags[f"chxf{cc % 2}"])
                    p.op("scalar", lambda e, x_=xch, o=xf, a=c6(sb_, 1): e.activation(
                        o, x_, AF.Square, accum_out=a), s_s, 1)
                v_sq = p.val(s_s)
                p.wait("vector", s_s, v_sq)
                p.drain("vector")
                for (cc, sb_, xch, xf, lnb, lo) in two:
                    p.op("vector", lambda e, s=c6(sb_, 0), o=c6(sb_, 2): e.tensor_scalar(
                        o, s, 1.0 / D, None, op0=ALU.mult))
                for (cc, sb_, xch, xf, lnb, lo) in two:
                    p.op("vector", lambda e, s=c6(sb_, 1), o=c6(sb_, 3): e.tensor_scalar(
                        o, s, 1.0 / D, None, op0=ALU.mult))
                for (cc, sb_, xch, xf, lnb, lo) in two:
                    p.op("vector", lambda e, m=c6(sb_, 2), q=c6(sb_, 3), o=c6(sb_, 4):
                         e.scalar_tensor_tensor(o, m, m, q, op0=ALU.mult, op1=ALU.subtract))
                for (cc, sb_, xch, xf, lnb, lo) in two:
                    p.op("vector", lambda e, v_=c6(sb_, 4): e.tensor_scalar(
                        v_, v_, -1.0, LN_EPS, op0=ALU.mult, op1=ALU.add),
                         s_v if cc == cbk else None, 1)
                v_var = p.val(s_v)
                p.wait("scalar", s_v, v_var)
                for (cc, sb_, xch, xf, lnb, lo) in two:
                    p.op("scalar", lambda e, v_=c6(sb_, 4), o=c6(sb_, 5): e.activation(
                        o, v_, AF.Sqrt), s_s if cc == cbk else None, 1)
                v_sd = p.val(s_s)
                p.wait("vector", s_s, v_sd)
                for (cc, sb_, xch, xf, lnb, lo) in two:
                    p.op("vector", lambda e, s=c6(sb_, 5), o=col(ST_r, cc): e.reciprocal(o, s))
                for (cc, sb_, xch, xf, lnb, lo) in two:
                    p.op("vector", lambda e, m=c6(sb_, 2), r=col(ST_r, cc), o=col(ST_b1, cc):
                         e.scalar_tensor_tensor(o, m, -1.0, r, op0=ALU.mult, op1=ALU.mult))
                for (cc, sb_, xch, xf, lnb, lo) in two:
                    p.op("vector", lambda e, r=col(ST_r, cc), o=col(ST_us, cc): e.tensor_scalar(
                        o, r, 1.0, None, op0=ALU.add), s_v if cc == cbk else None, 1)
                v_st1 = p.val(s_v)
                p.wait("scalar", s_v, v_st1)
                for (cc, sb_, xch, xf, lnb, lo) in two:
                    p.op("scalar", lambda e, x_=xch, o=xf, c=cc: e.activation(
                        o, x_, AF.Identity, bias=col(ST_b1, c), scale=col(ST_r, c)),
                         s_s, 1)
                    xslot_tag[cc % 4] = (s_s, p.val(s_s))
                v_xn = p.val(s_s)

                p.wait("vector", s_s, v_xn)
                for (cc, sb_, xch, xf, lnb, lo) in two:
                    p.op("vector", lambda e, x_=xf, o=c6(sb_, 0): e.reduce_sum(o, x_, axis=AX.X))
                for (cc, sb_, xch, xf, lnb, lo) in two:
                    if tags.get(f"chln{cc % 2}") is not None:
                        p.wait("scalar", *tags[f"chln{cc % 2}"])
                    p.op("scalar", lambda e, x_=xf, o=lnb, a=c6(sb_, 1): e.activation(
                        o, x_, AF.Square, accum_out=a), s_s, 1)
                v_sq2 = p.val(s_s)
                p.wait("vector", s_s, v_sq2)
                p.drain("vector")
                for (cc, sb_, xch, xf, lnb, lo) in two:
                    p.op("vector", lambda e, s=c6(sb_, 0), o=c6(sb_, 2): e.tensor_scalar(
                        o, s, 1.0 / D, None, op0=ALU.mult))
                for (cc, sb_, xch, xf, lnb, lo) in two:
                    p.op("vector", lambda e, s=c6(sb_, 1), o=c6(sb_, 3): e.tensor_scalar(
                        o, s, 1.0 / D, None, op0=ALU.mult))
                for (cc, sb_, xch, xf, lnb, lo) in two:
                    p.op("vector", lambda e, m=c6(sb_, 2), q=c6(sb_, 3), o=c6(sb_, 4):
                         e.scalar_tensor_tensor(o, m, m, q, op0=ALU.mult, op1=ALU.subtract))
                for (cc, sb_, xch, xf, lnb, lo) in two:
                    p.op("vector", lambda e, v_=c6(sb_, 4): e.tensor_scalar(
                        v_, v_, -1.0, LN_EPS, op0=ALU.mult, op1=ALU.add),
                         s_v if cc == cbk else None, 1)
                v_var2 = p.val(s_v)
                p.wait("scalar", s_v, v_var2)
                for (cc, sb_, xch, xf, lnb, lo) in two:
                    p.op("scalar", lambda e, v_=c6(sb_, 4), o=c6(sb_, 5): e.activation(
                        o, v_, AF.Sqrt), s_s if cc == cbk else None, 1)
                v_sd2 = p.val(s_s)
                p.wait("vector", s_s, v_sd2)
                for (cc, sb_, xch, xf, lnb, lo) in two:
                    p.op("vector", lambda e, s=c6(sb_, 5), o=col(ST_r2, cc): e.reciprocal(o, s))
                for (cc, sb_, xch, xf, lnb, lo) in two:
                    p.op("vector", lambda e, m=c6(sb_, 2), r=col(ST_r2, cc), o=col(ST_b2, cc):
                         e.scalar_tensor_tensor(o, m, -1.0, r, op0=ALU.mult, op1=ALU.mult),
                         s_v if cc == cbk else None, 1)
                v_st2 = p.val(s_v)
                p.wait("scalar", s_v, v_st2)
                for (cc, sb_, xch, xf, lnb, lo) in two:
                    p.op("scalar", lambda e, o=lnb, x_=xf, c=cc: e.activation(
                        o, x_, AF.Identity, bias=col(ST_b2, c), scale=col(ST_r2, c)), s_s, 1)
                v_ln2 = p.val(s_s)

                v_t_of = {}
                for (cc, sb_, xch, xf, lnb, lo) in two:
                    par2 = cc % 2
                    xb0, xb1 = (PS[0], PS[1]) if par2 == 0 else (PS[4], PS[5])
                    lnbk = PS[2] if par2 == 0 else PS[7]
                    if tags.get(f"ps01_{par2}") is not None:
                        p.wait("tensor", *tags[f"ps01_{par2}"])
                    p.wait("tensor", s_s, v_xn)
                    for k in range(8):
                        bk2 = (xb0, xb1)[k // 4]
                        p.op("tensor", lambda e, s=xf[:, k * 128:(k + 1) * 128],
                             d=bk2[:, (k % 4) * 128:(k % 4 + 1) * 128]: e.transpose(d, s, IDF[:]),
                             s_t if k == 7 else None, 1)
                    v_txn = p.val(s_t)
                    if tags.get(f"ps2_{par2}") is not None:
                        p.wait("tensor", *tags[f"ps2_{par2}"])
                    p.wait("tensor", s_s, v_ln2)
                    for k in range(8):
                        d = lnbk[:].bitcast(BF16)[:, k * 128:(k + 1) * 128]
                        p.op("tensor", lambda e, s=lnb[:, k * 128:(k + 1) * 128], d=d:
                             e.transpose(d, s, IDB[:]), s_t if k == 7 else None, 1)
                    v_t_of[cc] = (v_txn, p.val(s_t))

                for (cc, sb_, xch, xf, lnb, lo) in two:
                    par2 = cc % 2
                    xb0, xb1 = (PS[0], PS[1]) if par2 == 0 else (PS[4], PS[5])
                    lnbk = PS[2] if par2 == 0 else PS[7]
                    v_txn, v_tln = v_t_of[cc]
                    xnT_dst = AR1[:].rearrange("q (dc c) -> q dc c", c=2048)
                    p.wait("vector", s_t, v_txn)
                    for hb in range(2):
                        dst = xnT_dst[:, hb * 4:(hb + 1) * 4, cc * 128:(cc + 1) * 128]
                        src2 = (xb0, xb1)[hb][:].rearrange("q (k c) -> q k c", c=128)
                        p.op("vector", lambda e, d=dst, s=src2: e.tensor_copy(d, s))
                    p.drain("vector")
                    if lot_tag[cc % 2] is not None:
                        p.wait("vector", *lot_tag[cc % 2])
                    for hb in range(2):
                        xn_src = xnT_dst[:, hb * 4:(hb + 1) * 4, cc * 128:(cc + 1) * 128]
                        lod = lo.rearrange("q (k c) -> q k c", c=128)[:, hb * 4:(hb + 1) * 4, :]
                        src2 = (xb0, xb1)[hb][:].rearrange("q (k c) -> q k c", c=128)
                        p.op("vector", lambda e, d=lod, s=src2, x_=xn_src: e.tensor_tensor(
                            d, s, x_, op=ALU.subtract), s_v if hb == 1 else None, 1)
                    v_exn = p.val(s_v)
                    tags[f"ps01_{par2}"] = (s_v, v_exn)
                    tags[f"chxf{cc % 2}"] = (s_v, v_exn)

                    ln2T_dst = AR2[:].rearrange("q (dc c) -> q dc c", c=2048)
                    p.wait("vector", s_t, v_tln)
                    p.op("vector", lambda e, d=ln2T_dst[:, :, cc * 128:(cc + 1) * 128],
                         s=lnbk[:].bitcast(BF16).rearrange("q (k c) -> q k c", c=128): e.tensor_copy(d, s),
                         s_v, 1)
                    v_eln = p.val(s_v)
                    tags[f"ps2_{par2}"] = (s_v, v_eln)
                    tags[f"chln{cc % 2}"] = (s_v, v_eln)

                    p.wait("tensor", s_v, v_exn)
                    rbank = PS[3] if cc % 2 == 0 else PS[6]
                    if tags.get(f"rbank{cc % 2}") is not None:
                        p.wait("tensor", *tags[f"rbank{cc % 2}"])
                    first = True
                    ROUT_TERMS = [(True, True), (True, False), (False, True), (False, False)]
                    for dc in range(8):
                        for lhs_hi, rhs_hi in ROUT_TERMS:
                            lhs = (AR1[:, dc * 2048 + cc * 128: dc * 2048 + (cc + 1) * 128]
                                   if lhs_hi else lo[:, dc * 128:(dc + 1) * 128])
                            rhs = (WRH if rhs_hi else WRL)[:, dc * 8:(dc + 1) * 8]
                            is_last = (dc == 7 and (lhs_hi, rhs_hi) == ROUT_TERMS[-1])
                            p.op("tensor", lambda e, l=lhs, r=rhs, f=first, z=is_last, pb=rbank: e.matmul(
                                pb[:, 0:8], l, r, start=f, stop=z),
                                s_t if is_last else None, 1)
                            first = False
                    v_rmm = p.val(s_t)
                    lot_tag[cc % 2] = (s_t, v_rmm)
                    p.wait("vector", s_t, v_rmm)
                    p.op("vector", lambda e, c=cc, pb=rbank: e.tensor_copy(
                        RZ[:, c * 8:(c + 1) * 8], pb[:, 0:8]), s_v, 1)
                    tags[f"rbank{cc % 2}"] = (s_v, p.val(s_v))
            p.wait("scalar", s_v, p.val(s_v))
            if debug_taps and b == 0:
                p.wait("sync", s_v, p.val(s_v))
                p.op("sync", lambda e: e.dma_start(out=taps["ps6"], in_=RZ[:]), s_o, 16)
                p.op("sync", lambda e: e.dma_start(out=taps["rt"], in_=RT[:]), s_o, 16)
            p.op("scalar", lambda e: e.activation(inh, RZ[:], AF.Sigmoid, scale=RT[:, 0:1]), s_s, 1)

            if debug_taps and b == 0:
                p.wait("sync", s_t, v_rmm)
                p.op("sync", lambda e: e.dma_start(out=taps["lot"], in_=LOT[1].bitcast(F32)), s_o, 16)

            def mm_group(eng_waits, bank_ap, pairs, n_free):
                for (sm, vv) in eng_waits:
                    p.wait("tensor", sm, vv)
                n = len(pairs)
                for i, (l, r) in enumerate(pairs):
                    p.op("tensor",
                         lambda e, l=l, r=r, f=(i == 0), z=(i == n - 1),
                         o=bank_ap[:, 0:n_free]: e.matmul(o, l, r, start=f, stop=z),
                         s_t if i == n - 1 else None, 1)
                return p.val(s_t)

            ln2s = AR2
            WBv = [WB[i].rearrange("p (dc f) -> p dc f", f=1024) for i in range(2)]
            ROW_ONES128 = ROWS[0:1, 3072:3200]
            bank_tag = {}

            def bwait(key):
                return [bank_tag[key]] if key in bank_tag else []

            if tags.get("ar6_free") is not None:
                p.wait("gpsimd", *tags["ar6_free"])
            v_wv = p.op("gpsimd", lambda e: e.dma_start(
                out=WBv[0], in_=qkvT_r[:, :, 2048:3072]), s_wa, 16)
            if tags.get("ar6_free") is not None:
                p.wait("gpsimd", *tags["ar6_free"])
            v_wq = p.op("gpsimd", lambda e: e.dma_start(
                out=WBv[1], in_=qkvT_r[:, :, 0:1024]), s_wb, 16)

            def bias_section(wb_idx, sec, scale):
                for fh in range(2):
                    pairs = [(NBTB[:, dc:dc + 1],
                              WB[wb_idx][:, dc * 1024 + fh * 512: dc * 1024 + (fh + 1) * 512])
                             for dc in range(8)]
                    pairs.append((ROW_ONE64,
                                  ROW_QKVB[:, sec * 1024 + fh * 512: sec * 1024 + (fh + 1) * 512]))
                    vv = mm_group(bwait("bias"), PS[6][0:1, :], pairs, 512)
                    p.wait("vector", s_t, vv)
                    dst = ROW_BIAS[:, sec * 1024 + fh * 512: sec * 1024 + (fh + 1) * 512]
                    if scale is None:
                        p.op("vector", lambda e, d=dst: e.tensor_copy(d, PS[6][0:1, 0:512]), s_v, 1)
                    else:
                        p.op("vector", lambda e, d=dst, sc=scale: e.tensor_scalar(
                            d, PS[6][0:1, 0:512], sc, None, op0=ALU.mult), s_v, 1)
                    bank_tag["bias"] = (s_v, p.val(s_v))
                return vv

            def fold(wb_idx, nwt_tile, wait_sem, wait_v, bias_v=None):
                p.wait("vector", wait_sem, wait_v)
                if bias_v is not None:
                    p.wait("vector", s_t, bias_v)
                for dc in range(8):
                    sl = WB[wb_idx][:, dc * 1024:(dc + 1) * 1024]
                    p.op("vector", lambda e, s=sl, c=nwt_tile[:, dc:dc + 1]: e.tensor_scalar(
                        s, s, c, None, op0=ALU.mult), s_v if dc == 7 else None, 1)
                return p.val(s_v)

            bias_v_v = bias_v_q = None
            if b == 0:
                bwv = [tags["rbank1"]] if "rbank1" in tags else []
                for w in bwv:
                    p.wait("tensor", w[0], w[1])
                p.wait("tensor", s_v, v_pro_done)
                p.wait("tensor", s_wa, v_wv)
                bias_v_v = bias_section(0, 2, None)
                p.wait("tensor", s_wb, v_wq)
                bias_v_q = bias_section(1, 0, ATT_SCALE)
            v_fv = fold(0, NWT, s_wa, v_wv, bias_v_v)
            v_fq = fold(1, NWTQ, s_wb, v_wq, bias_v_q)

            inh = RS[:, 0:128]
            dist = RS[:, 128:256]
            wz = RS[:, 256:384]
            fb = RS[:, 384:512]
            t1 = RS[:, 512:640]
            t2 = RS[:, 640:768]

            def r3(t):
                return t.rearrange("q (a e) -> q a e", e=8)

            def rb(t16):
                return t16.unsqueeze(2).broadcast_to([128, 16, 8])

            p.wait("scalar", s_v, p.val(s_v))
            if debug_taps and b == 0:
                p.wait("sync", s_v, p.val(s_v))
                p.op("sync", lambda e: e.dma_start(out=taps["ps6"], in_=RZ[:]), s_o, 16)
                p.op("sync", lambda e: e.dma_start(out=taps["rt"], in_=RT[:]), s_o, 16)
            p.op("scalar", lambda e: e.activation(inh, RZ[:], AF.Sigmoid, scale=RT[:, 0:1]), s_s, 1)
            v_inh = p.val(s_s)
            p.wait("vector", s_s, v_inh)
            p.op("vector", lambda e: e.tensor_scalar(
                dist, inh, -G_CENTER, None, op0=ALU.add))
            p.drain("vector")
            p.op("vector", lambda e: e.tensor_scalar(
                t1, dist, -1.0, None, op0=ALU.mult))
            p.drain("vector")
            p.op("vector", lambda e: e.tensor_tensor(dist, dist, t1, op=ALU.max), s_v, 1)
            p.drain("vector")
            p.wait("scalar", s_v, p.val(s_v))
            p.op("scalar", lambda e: e.activation(wz, dist, AF.Exp, scale=-10.0))
            p.op("scalar", lambda e: e.activation(fb, dist, AF.Exp, scale=-10.0 / 3.0), s_s, 1)
            v_exps = p.val(s_s)
            p.wait("vector", s_s, v_exps)
            p.op("vector", lambda e: e.tensor_scalar(t1, inh, G_LOWER, None, op0=ALU.is_ge))
            p.drain("vector")
            p.op("vector", lambda e: e.tensor_scalar(t2, inh, G_UPPER, None, op0=ALU.is_le))
            p.drain("vector")
            p.op("vector", lambda e: e.tensor_tensor(t1, t1, t2, op=ALU.mult))
            p.drain("vector")
            p.op("vector", lambda e: e.tensor_tensor(wz, wz, t1, op=ALU.mult))
            p.drain("vector")
            p.op("vector", lambda e: e.reduce_sum(t2[:, 0:16], r3(wz), axis=AX.X))
            p.drain("vector")
            p.op("vector", lambda e: e.reduce_max(t2[:, 16:32], r3(fb), axis=AX.X))
            p.drain("vector")
            p.op("vector", lambda e: e.tensor_tensor(t1, fb, rb(t2[:, 16:32]), op=ALU.is_ge))
            p.drain("vector")
            p.op("vector", lambda e: e.scalar_tensor_tensor(dist, t1, -1e30, fb, op0=ALU.mult, op1=ALU.add))
            p.drain("vector")
            p.op("vector", lambda e: e.reduce_max(t2[:, 32:48], r3(dist), axis=AX.X))
            p.drain("vector")
            p.op("vector", lambda e: e.tensor_tensor(dist, dist, rb(t2[:, 32:48]), op=ALU.is_ge))
            p.drain("vector")
            p.op("vector", lambda e: e.tensor_tensor(t1, t1, dist, op=ALU.add))
            p.drain("vector")
            p.op("vector", lambda e: e.tensor_tensor(fb, fb, t1, op=ALU.mult))
            p.drain("vector")
            p.op("vector", lambda e: e.reduce_sum(t2[:, 48:64], r3(fb), axis=AX.X))
            p.drain("vector")
            p.op("vector", lambda e: e.tensor_scalar(t2[:, 48:64], t2[:, 48:64], 1e-8, None, op0=ALU.max))
            p.drain("vector")
            p.op("vector", lambda e: e.reciprocal(t2[:, 64:80], t2[:, 48:64]))
            p.drain("vector")
            p.op("vector", lambda e: e.tensor_tensor(fb, fb, rb(t2[:, 64:80]), op=ALU.mult))
            p.drain("vector")
            p.op("vector", lambda e: e.tensor_scalar(t2[:, 80:96], t2[:, 0:16], 1e-8, None, op0=ALU.is_lt))
            p.drain("vector")
            p.op("vector", lambda e: e.tensor_tensor(fb, fb, wz, op=ALU.subtract))
            p.drain("vector")
            p.op("vector", lambda e: e.tensor_tensor(fb, fb, rb(t2[:, 80:96]), op=ALU.mult))
            p.drain("vector")
            p.op("vector", lambda e: e.tensor_tensor(wz, wz, fb, op=ALU.add))
            p.drain("vector")
            p.op("vector", lambda e: e.reduce_sum(t2[:, 96:112], r3(wz), axis=AX.X))
            p.drain("vector")
            p.op("vector", lambda e: e.tensor_scalar(t2[:, 96:112], t2[:, 96:112], 1e-8, None, op0=ALU.max))
            p.drain("vector")
            p.op("vector", lambda e: e.reciprocal(t2[:, 112:128], t2[:, 96:112]))
            p.drain("vector")
            p.op("vector", lambda e: e.tensor_tensor(RW[:], wz, rb(t2[:, 112:128]), op=ALU.mult), s_v, 1)
            p.drain("vector")
            v_rout = p.val(s_v)

            if debug_taps == "A" and b == 0:
                p.wait("sync", s_v, p.val(s_v))
                p.wait("sync", s_s, p.val(s_s))
                p.op("sync", lambda e: e.dma_start(out=taps["xnT"], in_=AR1[:].bitcast(F32)), s_o, 16)
                p.op("sync", lambda e: e.dma_start(out=taps["ln2T"], in_=AR2[:].bitcast(F32)), s_o, 16)
                p.op("sync", lambda e: e.dma_start(out=taps["rout"], in_=RW[:]), s_o, 16)
                p.op("sync", lambda e: e.dma_start(out=taps["stats"], in_=ST_r[:]), s_o, 16)
                p.op("sync", lambda e: e.dma_start(out=taps["sumd"], in_=SUMD[:]), s_o, 16)
                p.op("sync", lambda e: e.dma_start(out=taps["inh"], in_=RS[:, 0:128]), s_o, 16)
                p.op("sync", lambda e: e.dma_start(out=taps["sqd"], in_=SQD[:]), s_o, 16)
                p.wait("sync", s_o, p.val(s_o))
                break

            for fk in range(8):
                for cb in range(4):
                    pairs = [(WB[0][:, dc * 1024 + fk * 128: dc * 1024 + (fk + 1) * 128],
                              ln2s[:, dc * 2048 + cb * 512: dc * 2048 + (cb + 1) * 512])
                             for dc in range(8)]
                    pairs.append((ROW_BIAS[:, 2048 + fk * 128: 2048 + (fk + 1) * 128],
                                  ROW_ONES))
                    bk = f"v{(fk * 4 + cb) % 2}"
                    waits = bwait(bk) + ([(s_v, v_fv)] if fk == 0 and cb == 0 else [])
                    vv = mm_group(waits, PS[(fk * 4 + cb) % 2], pairs, 512)
                    p.wait("scalar", s_t, vv)
                    if b > 0 and tags.get("ar3_free_s") is not None and fk == 0 and cb == 0:
                        p.wait("scalar", *tags["ar3_free_s"])
                    dst = AR3[:, fk * 2048 + cb * 512: fk * 2048 + (cb + 1) * 512]
                    p.op("scalar", lambda e, d=dst, pb=PS[(fk * 4 + cb) % 2]: e.activation(
                        d, pb[:, 0:512], AF.Identity), s_s, 1)
                    bank_tag[bk] = (s_s, p.val(s_s))
                    v_v_last = vv

            for cc in range(TC):
                for fj in range(2):
                    pairs = [(ln2s[:, dc * 2048 + cc * 128: dc * 2048 + (cc + 1) * 128],
                              WB[1][:, dc * 1024 + fj * 512: dc * 1024 + (fj + 1) * 512])
                             for dc in range(8)]
                    pairs.append((ROW_ONES128,
                                  ROW_BIAS[:, fj * 512:(fj + 1) * 512]))
                    bk = f"q{(cc * 2 + fj) % 2}"
                    waits = bwait(bk) + ([(s_v, v_fq), tags.get("rbank0", (s_v, 0))]
                                         if cc == 0 and fj == 0 else [])
                    vv = mm_group(waits, PS[2 + (cc * 2 + fj) % 2], pairs, 512)
                    p.wait("scalar", s_t, vv)
                    dst = AR4[:, cc * 1024 + fj * 512: cc * 1024 + (fj + 1) * 512]
                    p.op("scalar", lambda e, d=dst, pb=PS[2 + (cc * 2 + fj) % 2]: e.activation(
                        d, pb[:, 0:512], AF.Identity), s_s, 1)
                    bank_tag[bk] = (s_s, p.val(s_s))

            p.wait("gpsimd", s_t, v_v_last)
            v_wk = p.op("gpsimd", lambda e: e.dma_start(
                out=WBv[0], in_=qkvT_r[:, :, 1024:2048]), s_wa, 16)
            bias_v_k = None
            if b == 0:
                p.wait("tensor", s_wa, v_wk)
                bias_v_k = bias_section(0, 1, None)
            v_fk = fold(0, NWT, s_wa, v_wk, bias_v_k)
            for cc in range(TC):
                for fj in range(2):
                    pairs = [(ln2s[:, dc * 2048 + cc * 128: dc * 2048 + (cc + 1) * 128],
                              WB[0][:, dc * 1024 + fj * 512: dc * 1024 + (fj + 1) * 512])
                             for dc in range(8)]
                    pairs.append((ROW_ONES128,
                                  ROW_BIAS[:, 1024 + fj * 512: 1024 + (fj + 1) * 512]))
                    bk = f"k{(cc * 2 + fj) % 2}"
                    waits = bwait(bk) + ([(s_v, v_fk)] if cc == 0 and fj == 0 else [])
                    vv = mm_group(waits, PS[4 + (cc * 2 + fj) % 2], pairs, 512)
                    p.wait("scalar", s_t, vv)
                    if b > 0 and cc == 0 and fj == 0 and tags.get("acc_out") is not None:
                        p.wait("scalar", *tags["acc_out"])
                    dst = AR5[:, cc * 1024 + fj * 512: cc * 1024 + (fj + 1) * 512]
                    p.op("scalar", lambda e, d=dst, pb=PS[4 + (cc * 2 + fj) % 2]: e.activation(
                        d, pb[:, 0:512], AF.Identity), s_s, 1)
                    bank_tag[bk] = (s_s, p.val(s_s))
            v_qkv_evacs = p.val(s_s)

            p.wait("tensor", s_s, v_qkv_evacs)
            v_s_last = {}
            for qi in range(8):
                for kh in range(2):
                    pairs = [(AR4[:, cc2 * 1024 + qi * 128: cc2 * 1024 + (qi + 1) * 128],
                              AR5[:, cc2 * 1024 + kh * 512: cc2 * 1024 + (kh + 1) * 512])
                             for cc2 in range(TC)]
                    bk = f"s{(qi % 2) * 2 + kh}"
                    vv = mm_group(bwait(bk), PS[(qi % 2) * 2 + kh], pairs, 512)
                    v_s_last[(qi, kh)] = vv
                pa = PS[(qi % 2) * 2]
                pb_ = PS[(qi % 2) * 2 + 1]
                p.wait("vector", s_t, v_s_last[(qi, 1)])
                p.op("vector", lambda e, q=qi, pb2=pa: e.reduce_max(
                    col(SMX, q * 2), pb2[:, 0:512], axis=AX.X))
                p.op("vector", lambda e, q=qi, pb2=pb_: e.reduce_max(
                    col(SMX, q * 2 + 1), pb2[:, 0:512], axis=AX.X))
                p.drain("vector")
                p.op("vector", lambda e, q=qi: e.tensor_tensor(
                    col(ST_t, 6), col(SMX, q * 2), col(SMX, q * 2 + 1), op=ALU.max))
                p.drain("vector")
                p.op("vector", lambda e, q=qi: e.tensor_scalar(
                    col(SNMX, q), col(ST_t, 6), -1.0, None, op0=ALU.mult), s_v, 1)
                v_nmx = p.val(s_v)
                p.wait("scalar", s_v, v_nmx)
                if b > 0 and qi == 0 and tags.get("acc_out") is not None:
                    p.wait("scalar", *tags["acc_out"])
                for kh in range(2):
                    pbx = PS[(qi % 2) * 2 + kh]
                    dst = AR6f[:, qi * 1024 + kh * 512: qi * 1024 + (kh + 1) * 512]
                    p.op("scalar", lambda e, d=dst, pb2=pbx, q=qi, k2=kh: e.activation(
                        d, pb2[:, 0:512], AF.Exp, bias=col(SNMX, q),
                        accum_out=col(SSUM, q * 2 + k2)), s_s, 1)
                v_exp = p.val(s_s)
                bank_tag[f"s{(qi % 2) * 2}"] = (s_s, v_exp)
                bank_tag[f"s{(qi % 2) * 2 + 1}"] = (s_s, v_exp)
                p.wait("vector", s_s, v_exp)
                p.op("vector", lambda e, q=qi: e.tensor_tensor(
                    col(ST_t, 6), col(SSUM, q * 2), col(SSUM, q * 2 + 1), op=ALU.add))
                p.drain("vector")
                p.op("vector", lambda e, q=qi: e.reciprocal(col(SRCP, q), col(ST_t, 6)), s_v, 1)
            v_smax_done = p.val(s_v)
            v_s_groups_done = v_s_last[(7, 1)]

            AR2v = AR2[:].rearrange("pq (ki q) -> pq ki q", q=1024)
            v_exp_all = p.val(s_s)
            p.wait("tensor", s_s, v_exp_all)
            for qi in range(8):
                for kg in range(2):
                    bk = f"t{kg}"
                    for w in bwait(bk):
                        p.wait("tensor", w[0], w[1])
                    pbank = PS[4 + kg]
                    for j in range(4):
                        src = AR6f[:, qi * 1024 + (kg * 4 + j) * 128: qi * 1024 + (kg * 4 + j + 1) * 128]
                        p.op("tensor", lambda e, s2=src, d=pbank[:, j * 128:(j + 1) * 128]:
                             e.transpose(d, s2, IDF[:]), s_t if j == 3 else None, 1)
                    vv = p.val(s_t)
                    p.wait("vector", s_t, vv)
                    dst = AR2v[:, kg * 4:(kg + 1) * 4, qi * 128:(qi + 1) * 128]
                    src3 = pbank[:].rearrange("pq (k c2) -> pq k c2", c2=128)
                    p.op("vector", lambda e, d=dst, s2=src3: e.tensor_copy(d, s2), s_v, 1)
                    bank_tag[bk] = (s_v, p.val(s_v))
            v_swt_done = p.val(s_v)

            p.wait("gpsimd", s_t, v_s_groups_done)
            v_wp = p.op("gpsimd", lambda e: e.dma_start(
                out=WPJ.rearrange("pq (dc f) -> pq dc f", f=1024), in_=projT_r[:]), s_wp, 16)

            xr_tag = [None, None]
            xr_v = [0, 0]
            u_tag = [None, None]
            for cb in range(4):
                for qi in range(8):
                    pairs = [(AR2[:, ki * 1024 + qi * 128: ki * 1024 + (qi + 1) * 128],
                              AR3[:, ki * 2048 + cb * 512: ki * 2048 + (cb + 1) * 512])
                             for ki in range(8)]
                    bk = f"a{qi % 2}"
                    waits = bwait(bk)
                    if cb == 0 and qi == 0:
                        waits = waits + [(s_v, v_swt_done)]
                    if qi == 0 and cb >= 2:
                        pass
                    vv = mm_group(waits, PS[qi % 2], pairs, 512)
                    p.wait("scalar", s_t, vv)
                    p.wait("scalar", s_v, v_smax_done)
                    dst = ATB[cb % 2][:, qi * 512:(qi + 1) * 512]
                    p.op("scalar", lambda e, d=dst, pb2=PS[qi % 2], q=qi: e.activation(
                        d, pb2[:, 0:512], AF.Identity, scale=col(SRCP, q)), s_s, 1)
                    bank_tag[bk] = (s_s, p.val(s_s))
                    v_A_mm_last = vv
                v_atb = p.val(s_s)

                for cj in range(4):
                    cc2 = cb * 4 + cj
                    sl = cj % 2
                    if xr_tag[sl] is not None:
                        p.wait("sync", *xr_tag[sl])
                    if cb == 0 and cj < 2:
                        p.wait("sync", s_t, v_s_groups_done)
                    xr_v[sl] = p.op("sync", lambda e, d=XRS[sl], s2=x_r[b, :, cc2, :]:
                                    e.dma_start(out=d, in_=s2), s_xr[sl], 16)
                    p.wait("scalar", s_xr[sl], xr_v[sl])
                    if u_tag[sl] is not None:
                        p.wait("scalar", *u_tag[sl])
                    p.op("scalar", lambda e, d=URS[sl], s2=XRS[sl], c=cc2: e.activation(
                        d, s2, AF.Identity, bias=col(ST_b1, c), scale=col(ST_us, c)), s_s, 1)
                    v_u = p.val(s_s)
                    xr_tag[sl] = (s_s, v_u)

                    for dj in range(2):
                        pairs = [(ATB[cb % 2][:, qi * 512 + cj * 128: qi * 512 + cj * 128 + 128],
                                  WPJ[:, qi * 1024 + dj * 512: qi * 1024 + (dj + 1) * 512])
                                 for qi in range(8)]
                        pairs.append((ROW_ONES128,
                                      ROW_PROJB[:, dj * 512:(dj + 1) * 512]))
                        bk = f"p{(cj * 2 + dj) % 2}"
                        waits = bwait(bk) + [(s_s, v_atb)]
                        if cb == 0 and cj == 0 and dj == 0:
                            waits.append((s_wp, v_wp))
                        vv = mm_group(waits, PS[2 + (cj * 2 + dj) % 2], pairs, 512)
                        p.wait("vector", s_t, vv)
                        p.wait("vector", s_s, v_u)
                        if b > 0 and tags.get("acc_out") is not None and cb == 0 and cj == 0 and dj == 0:
                            p.wait("vector", *tags["acc_out"])
                        av = accv(cc2)
                        p.op("vector", lambda e, a=av[:, dj * 512:(dj + 1) * 512],
                             pb2=PS[2 + (cj * 2 + dj) % 2], us=URS[sl][:, dj * 512:(dj + 1) * 512]:
                             e.tensor_tensor(a, pb2[:, 0:512], us, op=ALU.add),
                             s_v, 1)
                        bank_tag[bk] = (s_v, p.val(s_v))
                    u_tag[sl] = (s_v, p.val(s_v))
            v_acc_init = p.val(s_v)

            v_last_A = v_A_mm_last
            if skip_ffn:
                v_ffn_done = v_acc_init
                v_gate_last = v_last_A
                v_eout_groups = v_last_A
            gwait = [(s_t, v_last_A)]
            for kk in ("g0", "g1", "u0", "u1", "e0", "e1", "e2", "e3"):
                bank_tag[kk] = (s_v, v_acc_init)
            v_wg_l = v_wu_l = v_wd_l = 0
            GSv = GS.rearrange("pq (dc f) -> pq dc f", f=1024)
            USv = US.rearrange("pq (dc f) -> pq dc f", f=1024)
            DSv = DS.rearrange("pq (dc f) -> pq dc f", f=1024)
            for e_i in range(E if not skip_ffn else 0):
                if e_i == 0:
                    for (sm, vv2) in gwait:
                        p.wait("gpsimd", sm, vv2)
                    v_wg_l = p.op("gpsimd", lambda e, i=e_i: e.dma_start(
                        out=GSv, in_=wg_r[i]), s_wg, 16)
                    v_wu_l = p.op("gpsimd", lambda e, i=e_i: e.dma_start(
                        out=USv, in_=wu_r[i]), s_wg, 16)
                    v_wd_l = p.op("gpsimd", lambda e, i=e_i: e.dma_start(
                        out=DSv, in_=wd_r[i]), s_wd, 16)
                v_gate_last = 0
                v_h_cb = [0] * 4

                def emit_G(cb):
                    nonlocal v_gate_last
                    hb = HB[cb % 2]
                    for ic in range(8):
                        par = ic % 2
                        pairs_g = [(GS[:, dc * 1024 + ic * 128: dc * 1024 + (ic + 1) * 128],
                                    AR1[:, dc * 2048 + cb * 512: dc * 2048 + (cb + 1) * 512])
                                   for dc in range(8)]
                        pairs_u = [(US[:, dc * 1024 + ic * 128: dc * 1024 + (ic + 1) * 128],
                                    AR1[:, dc * 2048 + cb * 512: dc * 2048 + (cb + 1) * 512])
                                   for dc in range(8)]
                        waits = bwait(f"g{par}")
                        if e_i == 0 and cb == 0 and ic == 0:
                            waits = waits + [(s_wg, v_wu_l)]
                        vg = mm_group(waits, PS[par * 2], pairs_g, 512)
                        vu = mm_group(bwait(f"u{par}"), PS[par * 2 + 1], pairs_u, 512)
                        v_gate_last = vu
                        p.wait("scalar", s_t, vg)
                        if ic < 2 and tags.get(f"hslot{cb % 2}") is not None:
                            p.wait("scalar", *tags[f"hslot{cb % 2}"])
                        p.op("scalar", lambda e, d=hb[:, ic * 512:(ic + 1) * 512],
                             pb2=PS[par * 2]: e.activation(d, pb2[:, 0:512], AF.Silu), s_s, 1)
                        v_si = p.val(s_s)
                        bank_tag[f"g{par}"] = (s_s, v_si)
                        p.wait("vector", s_s, v_si)
                        p.wait("vector", s_t, vu)
                        p.op("vector", lambda e, d=hb[:, ic * 512:(ic + 1) * 512],
                             pb2=PS[par * 2 + 1]: e.tensor_tensor(
                             d, d, pb2[:, 0:512], op=ALU.mult), s_v, 1)
                        v_h = p.val(s_v)
                        v_h_cb[cb] = v_h
                        bank_tag[f"u{par}"] = (s_v, v_h)

                def emit_E(cb):
                    hb = HB[cb % 2]
                    for cj in range(4):
                        cc2 = cb * 4 + cj
                        for dj in range(2):
                            pairs = [(hb[:, ic2 * 512 + cj * 128: ic2 * 512 + cj * 128 + 128],
                                      DS[:, ic2 * 1024 + dj * 512: ic2 * 1024 + (dj + 1) * 512])
                                     for ic2 in range(8)]
                            bk = f"e{(cj * 2 + dj) % 4}"
                            waits = bwait(bk) + [(s_v, v_h_cb[cb])]
                            if cb == 0 and cj == 0 and dj == 0:
                                waits.append((s_wd, v_wd_l))
                            vv = mm_group(waits, PS[4 + (cj * 2 + dj) % 4], pairs, 512)
                            p.wait("vector", s_t, vv)
                            av = accv(cc2)
                            p.op("vector", lambda e, a=av[:, dj * 512:(dj + 1) * 512],
                                 pb2=PS[4 + (cj * 2 + dj) % 4], c=cc2, i=e_i:
                                 e.scalar_tensor_tensor(
                                     a, pb2[:, 0:512], RW[:, c * 8 + i: c * 8 + i + 1], a,
                                     op0=ALU.mult, op1=ALU.add), s_v, 1)
                            bank_tag[bk] = (s_v, p.val(s_v))
                    tags[f"hslot{cb % 2}"] = (s_t, p.val(s_t))

                emit_G(0)
                emit_G(1)
                emit_E(0)
                emit_G(2)
                emit_E(1)
                emit_G(3)
                emit_E(2)
                emit_E(3)
                v_eout_groups = p.val(s_t)
                if e_i + 1 < E:
                    p.wait("gpsimd", s_t, v_gate_last)
                    v_wg_l = p.op("gpsimd", lambda e, i=e_i + 1: e.dma_start(
                        out=GSv, in_=wg_r[i]), s_wg, 16)
                    v_wu_l = p.op("gpsimd", lambda e, i=e_i + 1: e.dma_start(
                        out=USv, in_=wu_r[i]), s_wg, 16)
                    p.wait("gpsimd", s_t, v_eout_groups)
                    v_wd_l = p.op("gpsimd", lambda e, i=e_i + 1: e.dma_start(
                        out=DSv, in_=wd_r[i]), s_wd, 16)
                    bank_tag["g0"] = (s_wg, v_wu_l)
            if not skip_ffn:
                v_ffn_done = p.val(s_v)

            if b + 1 < nb:
                pre = []
                for sl in range(4):
                    if xslot_tag[sl] is not None:
                        p.wait("sync", *xslot_tag[sl])
                    p.wait("sync", s_v, v_acc_init)
                    pre.append(p.op(
                        "sync",
                        lambda e, d=XCH[sl], s=x_r[b + 1, :, sl, :]: e.dma_start(out=d, in_=s),
                        s_xc[sl], 16))
                tags["pre_x"] = pre

            p.wait("sync", s_v, v_ffn_done)
            p.op("sync", lambda e, bb=b: e.dma_start(
                out=out_r[bb, :, 0:8, :],
                in_=AR5f[:].rearrange("pq (n m) -> pq n m", m=1024)), s_o, 16)
            p.op("sync", lambda e, bb=b: e.dma_start(
                out=out_r[bb, :, 8:16, :],
                in_=AR6f[:].rearrange("pq (n m) -> pq n m", m=1024)), s_o, 16)
            v_out = p.val(s_o)
            tags["acc_out"] = (s_o, v_out)
            tags["ar6_free"] = (s_o, v_out)
            tags["ar4_free"] = (s_v, v_acc_init)
            tags["ar3_free_s"] = (s_t, v_gate_last)
            tags["ar1_free"] = (s_t, v_gate_last)
            tags["ar2_free"] = (s_t, v_eout_groups)

            if debug_taps == "B" and b == 0:
                p.wait("sync", s_o, p.val(s_o))
                break

        p.wait("sync", s_o, p.val(s_o))

        with nc.Block() as block:
            @block.sync
            def _(h):
                p.replay("sync", h)

            @block.gpsimd
            def _(h):
                p.replay("gpsimd", h)

            @block.tensor
            def _(h):
                p.replay("tensor", h)

            @block.vector
            def _(h):
                p.replay("vector", h)

            @block.scalar
            def _(h):
                p.replay("scalar", h)

    return nc


def host_inputs(inputs, core):
    x = np.asarray(inputs["L0_attn_x"])
    m = {
        "x": np.ascontiguousarray(x.reshape(N_CORES, NB, T, D)[core]),
        "qkvT": np.ascontiguousarray(np.asarray(inputs["L0_attn_qkv_w"]).T),
        "projT": np.ascontiguousarray(np.asarray(inputs["L0_attn_proj_w"]).T),
        "wg": np.ascontiguousarray(np.asarray(inputs["L0_ffn_Wg"])),
        "wu": np.ascontiguousarray(np.asarray(inputs["L0_ffn_Wu"])),
        "wd": np.ascontiguousarray(np.asarray(inputs["L0_ffn_Wd"])),
        "wrT": np.ascontiguousarray(np.asarray(inputs["L0_ffn_Wr"]).T),
        "normw": np.asarray(inputs["L0_attn_norm_w"]).reshape(1, D).copy(),
        "normb": np.asarray(inputs["L0_attn_norm_b"]).reshape(1, D).copy(),
        "qkvb": np.asarray(inputs["L0_attn_qkv_b"]).reshape(1, 3 * D).copy(),
        "projb": np.asarray(inputs["L0_attn_proj_b"]).reshape(1, D).copy(),
        "temp": np.asarray(inputs["L0_ffn_temperature"]).reshape(1, 1).copy(),
        "idenb": np.eye(128, dtype=ml_dtypes.bfloat16),
        "idenf": np.eye(128, dtype=np.float32),
        "onesb": np.ones((1, 512), dtype=ml_dtypes.bfloat16),
        "onesf": np.ones((1, 128), dtype=np.float32),
    }
    return m


def kernel(**inputs):
    nc = build_nc(NB, debug_taps=DEBUG_TAPS)
    in_maps = [host_inputs(inputs, c) for c in range(N_CORES)]
    res = run_bass_kernel_spmd(nc, in_maps, core_ids=list(range(N_CORES)))
    out = np.concatenate([np.asarray(r["out"]) for r in res.results], axis=0)
    return out.reshape(32, T, D)
